# version 10
# baseline (speedup 1.0000x reference)
import numpy as np
from contextlib import ExitStack

import concourse.bass as bass
import concourse.mybir as mybir
from concourse.bass_utils import run_bass_kernel_spmd

F32 = mybir.dt.float32
BF16 = mybir.dt.bfloat16
AF = mybir.ActivationFunctionType
ALU = mybir.AluOpType
AX = mybir.AxisListType

T = 4096
D = 1024
NP = 8
CT = 8
FT = 16
DT = 8
E9 = 9
NCH = 8
TT = 32
FCH = 8
FW = 512

_DEBUG_TAPS = False


class _Sem:
    def __init__(self, st, nc, name, step=1):
        self.h = st.enter_context(nc.semaphore(name))
        self.n = 0
        self.step = step

    def inc(self, instr):
        instr.then_inc(self.h, self.step)
        self.n += self.step
        return self.n


class _Waiter:
    def __init__(self):
        self.seen = {}

    def __call__(self, eng, sem, val=None):
        v = sem.n if val is None else val
        if v <= 0:
            return
        k = (id(eng), id(sem))
        if self.seen.get(k, 0) < v:
            eng.wait_ge(sem.h, v)
            self.seen[k] = v


class _Auto:

    def __init__(self, eng, sem, waiter, skip=("wait_ge", "dma_start")):
        self._e = eng
        self._s = sem
        self._W = waiter
        self._skip = set(skip)

    def __getattr__(self, name):
        f = getattr(self._e, name)
        if name in self._skip or name.startswith("_"):
            return f

        def g(*a, **k):
            self._W(self._e, self._s, self._s.n)
            ins = f(*a, **k)
            self._s.n += 1
            ins.then_inc(self._s.h, 1)
            return self._s.n

        return g


def build_program(dbg=False, phase_limit=99):
    nc = bass.Bass("TRN2", target_bir_lowering=False, debug=False)
    tn, sl, ve, gp, sy = nc.tensor, nc.scalar, nc.vector, nc.gpsimd, nc.sync

    x_d = nc.dram_tensor("x", [T, D], F32, kind="ExternalInput")
    wq_d = nc.dram_tensor("wq_t", [NP, 128, CT, 128], F32, kind="ExternalInput")
    wk_d = nc.dram_tensor("wk_t", [NP, 128, CT, 128], F32, kind="ExternalInput")
    wv_d = nc.dram_tensor("wv_t", [NP, 128, CT, 128], F32, kind="ExternalInput")
    wo_d = nc.dram_tensor("wo_t", [128, CT, DT, 128], F32, kind="ExternalInput")
    bq_d = nc.dram_tensor("bq_t", [128, 16], F32, kind="ExternalInput")
    bk_d = nc.dram_tensor("bk_t", [128, 16], F32, kind="ExternalInput")
    bv_d = nc.dram_tensor("bv_t", [128, 16], F32, kind="ExternalInput")
    bo_d = nc.dram_tensor("bo_t", [128, 8], F32, kind="ExternalInput")
    s4_d = nc.dram_tensor("s4", [1, 4], F32, kind="ExternalInput")
    rw1_d = nc.dram_tensor("rw1_t", [E9, 128, FT, CT, 128], F32, kind="ExternalInput")
    rw2_d = nc.dram_tensor("rw2_t", [E9, 128, DT, FT, 128], F32, kind="ExternalInput")
    rb1_d = nc.dram_tensor("rb1_t", [128, E9, FT], F32, kind="ExternalInput")
    rb2_d = nc.dram_tensor("rb2s", [E9, D], F32, kind="ExternalInput")
    rwr_d = nc.dram_tensor("rwr_t", [128, CT, 8], F32, kind="ExternalInput")
    rbr_d = nc.dram_tensor("rbr_t", [8, 1], F32, kind="ExternalInput")
    sel_d = nc.dram_tensor("sel9", [E9, 8 * 128], F32, kind="ExternalInput")
    out_d = nc.dram_tensor("out", [T, D], F32, kind="ExternalOutput")
    scrk = "ExternalOutput" if dbg else "Internal"
    ffn_d = nc.dram_tensor("ffn_scr", [DT, 128, T], F32, kind=scrk)
    ctx_d = nc.dram_tensor("ctx_scr", [16, 64, T], BF16, kind=scrk)
    rw1b_d = nc.dram_tensor("rw1b", [E9, 128, FT, CT, 128], BF16, kind="Internal")
    rw2b_d = nc.dram_tensor("rw2b", [E9, 128, DT, FT, 128], BF16, kind="Internal")
    taps = {}
    if dbg:
        taps["d_xnT"] = nc.dram_tensor("d_xnT", [128, CT, T], BF16, kind="ExternalOutput")
        taps["d_W"] = nc.dram_tensor("d_W", [E9, T], BF16, kind="ExternalOutput")

    W = _Waiter()
    with ExitStack() as st:
        e = st.enter_context
        tS = _Sem(st, nc, "tS")
        sS = _Sem(st, nc, "sS")
        vS = _Sem(st, nc, "vS")
        gS = _Sem(st, nc, "gS")
        dX = [_Sem(st, nc, f"dX{i}", 16) for i in range(2)]
        dR = [_Sem(st, nc, f"dR{i}", 16) for i in range(2)]
        dSP = _Sem(st, nc, "dSP", 16)
        dFB = _Sem(st, nc, "dFB", 16)
        dCX = _Sem(st, nc, "dCX", 16)
        dOut = [_Sem(st, nc, f"dOut{i}", 16) for i in range(2)]
        dR2 = [_Sem(st, nc, f"dR2{i}", 16) for i in range(2)]
        dDBG = _Sem(st, nc, "dDBG", 16)
        dG = _Sem(st, nc, "dG", 16)
        dRB = _Sem(st, nc, "dRB", 16)
        dW1 = [_Sem(st, nc, f"dW1{i}", 16) for i in range(2)]
        dPC = _Sem(st, nc, "dPC", 16)
        sync_sems = dX + dR + [dSP, dFB, dCX] + dOut + [dDBG]
        gp_sems = [dG, dRB, dPC] + dW1 + dR2

        def wait_all_dma(eng):
            for s_ in sync_sems + gp_sems:
                W(eng, s_, s_.n)

        def dma(q, qsem, out, in_):
            q.dma_start(out=out, in_=in_).then_inc(qsem.h, 16)
            qsem.n += 16
            return qsem.n

        vv = _Auto(ve, vS, W)
        ss = _Auto(sl, sS, W)
        gg = _Auto(gp, gS, W)

        ident = e(nc.sbuf_tensor("ident", [128, 128], BF16))
        identf = e(nc.sbuf_tensor("identf", [128, 128], F32))
        eps = e(nc.sbuf_tensor("eps", [128, 1], F32))
        ones_r128 = e(nc.sbuf_tensor("ones_r128", [1, 128], F32))
        ones_all = e(nc.sbuf_tensor("ones_all", [128, 64], F32))
        scol = e(nc.sbuf_tensor("scol", [128, 4], F32))
        s_sb = e(nc.sbuf_tensor("s_sb", [1, 4], F32))
        bq_sb = e(nc.sbuf_tensor("bq_sb", [128, 16], F32))
        bk_sb = e(nc.sbuf_tensor("bk_sb", [128, 16], F32))
        bv_sb = e(nc.sbuf_tensor("bv_sb", [128, 16], F32))
        bo_sb = e(nc.sbuf_tensor("bo_sb", [128, 8], F32))
        rwr_sb = e(nc.sbuf_tensor("rwr_sb", [128, CT, 8], BF16))
        rbr_sb = e(nc.sbuf_tensor("rbr_sb", [8, 1], F32))

        for dst, src in [(s_sb, s4_d), (bq_sb, bq_d), (bk_sb, bk_d), (bv_sb, bv_d),
                         (bo_sb, bo_d), (rbr_sb, rbr_d), (rwr_sb, rwr_d)]:
            dma(gp, dG, dst[:], src[:])
        d_pre = dG.n
        gg.memset(ident[:], 0.0)
        gg.affine_select(out=ident[:], in_=ident[:], compare_op=ALU.not_equal,
                         fill=1.0, base=0, pattern=[[-1, 128]], channel_multiplier=1)
        gg.memset(identf[:], 0.0)
        gg.affine_select(out=identf[:], in_=identf[:], compare_op=ALU.not_equal,
                         fill=1.0, base=0, pattern=[[-1, 128]], channel_multiplier=1)
        gg.memset(eps[:], 1e-5)
        gg.memset(ones_r128[:], 1.0)
        g_pre = gg.memset(ones_all[:], 1.0)

        with ExitStack() as stM:
            eM = stM.enter_context
            xnT = eM(nc.sbuf_tensor("xnT", [128, CT, T], BF16))
            W_aug = eM(nc.sbuf_tensor("W_aug", [E9, T], BF16))
            rb1_sb = eM(nc.sbuf_tensor("rb1_sb", [128, E9, FT], F32))
            rb2_sb = eM(nc.sbuf_tensor("rb2_sb", [E9, D], BF16))
            sel_sb = eM(nc.sbuf_tensor("sel_sb", [E9, 8 * 128], BF16))
            W(gp, dG, d_pre)
            dma(gp, dG, rb1_sb[:], rb1_d[:])
            dma(gp, dG, rb2_sb[:], rb2_d[:])
            dma(gp, dG, sel_sb[:], sel_d[:])
            d_pre2 = dG.n
            W(gp, gS, g_pre)
            g_wa = gg.memset(W_aug[:], 1.0)

            with ExitStack() as st0:
                e0 = st0.enter_context
                xt2 = e0(nc.sbuf_tensor("p0_x", [128, 2, D], F32))
                xnb = e0(nc.sbuf_tensor("p0_xn", [128, 2, D], BF16))
                st0_ = e0(nc.sbuf_tensor("p0_stat", [128, 8], F32))
                bns = e0(nc.sbuf_tensor("p0_bns", [128, 12], F32))
                pbf = e0(nc.psum_tensor("p0_pbf", [128, 2, 1024], BF16))
                ppre = e0(nc.psum_tensor("p0_pre", [128, 8], F32))

                W(tn, dG, d_pre)
                W(tn, gS, g_pre)
                for i in range(4):
                    ins = tn.matmul(ppre[:, i:i + 1], ones_r128[:], s_sb[0:1, i:i + 1],
                                    start=True, stop=True)
                t_pre = tS.inc(ins)
                W(ve, tS, t_pre)
                W(ve, dG, d_pre)
                vv.tensor_copy(out=scol[:], in_=ppre[:, 0:4])
                vv.tensor_scalar(out=scol[:, 0:1], in0=scol[:, 0:1], scalar1=0.125,
                                 scalar2=None, op0=ALU.mult)
                v_pre = vv.tensor_scalar(out=bq_sb[:], in0=bq_sb[:], scalar1=0.125,
                                         scalar2=None, op0=ALU.mult)

                v_xn = [0] * TT
                v_cp = [0] * TT
                d_x = [0] * TT
                d_x[0] = dma(sy, dX[0], xt2[:, 0, :], x_d[0:128, :])
                d_x[1] = dma(sy, dX[1], xt2[:, 1, :], x_d[128:256, :])
                v_stat = [0] * TT
                t_trs = [0] * TT
                for i in range(TT):
                    xt = xt2[:, i % 2, :]
                    if i >= 2:
                        W(ve, tS, t_trs[i - 2])
                        v_cp[i - 2] = vv.tensor_copy(
                            out=xnT[:, :, 128 * (i - 2):128 * (i - 1)],
                            in_=pbf[:, (i - 2) % 2, :]
                            .rearrange("p (c t) -> p c t", c=CT))
                    W(ve, dX[i % 2], d_x[i])
                    W(ve, vS, v_pre)
                    vv.bn_stats(out=bns[:, 0:6], in_=xt[:, 0:512])
                    vv.bn_stats(out=bns[:, 6:12], in_=xt[:, 512:1024])
                    vv.bn_aggr(out=st0_[:, 0:2], in_=bns[:])
                    v_st = vv.tensor_scalar(out=st0_[:, 4:5], in0=st0_[:, 0:1],
                                            scalar1=-1.0, scalar2=None,
                                            op0=ALU.mult)
                    v_stat[i] = v_st
                    W(sl, vS, v_st)
                    s_r = 0
                    ss.activation(out=st0_[:, 5:6], in_=st0_[:, 1:2], func=AF.Ln,
                                  bias=eps[:], scale=1.0)
                    s_r = ss.activation(out=st0_[:, 6:7], in_=st0_[:, 5:6],
                                        func=AF.Exp, scale=-0.5)
                    W(ve, sS, s_r)
                    v_xn[i] = vv.tensor_scalar(out=xnb[:, i % 2, :], in0=xt,
                                               scalar1=st0_[:, 4:5],
                                               scalar2=st0_[:, 6:7],
                                               op0=ALU.add, op1=ALU.mult)
                    if i + 2 < TT:
                        W(sy, vS, v_xn[i])
                        d_x[i + 2] = dma(sy, dX[i % 2], xt2[:, i % 2, :],
                                         x_d[128 * (i + 2):128 * (i + 3), :])
                    W(tn, vS, v_xn[i])
                    if i >= 2:
                        W(tn, vS, v_cp[i - 2])
                    for j in range(CT):
                        ins = tn.transpose(pbf[:, i % 2, 128 * j:128 * (j + 1)],
                                           xnb[:, i % 2, 128 * j:128 * (j + 1)],
                                           ident[:])
                    t_trs[i] = tS.inc(ins)
                for i in (TT - 2, TT - 1):
                    W(ve, tS, t_trs[i])
                    v_cp[i] = vv.tensor_copy(
                        out=xnT[:, :, 128 * i:128 * (i + 1)],
                        in_=pbf[:, i % 2, :].rearrange("p (c t) -> p c t", c=CT))
                v_ln_done = v_cp[TT - 1]

            if dbg:
                W(sy, vS, v_ln_done)
                W(sy, dDBG, dDBG.n)
                dma(sy, dDBG, taps["d_xnT"][:], xnT[:])
            if phase_limit < 1:
                wait_all_dma(sy)
                wait_all_dma(gp)
                return nc

            with ExitStack() as stR:
                eR = stR.enter_context
                rre = eR(nc.sbuf_tensor("rre", [8, 512], F32))
                rt_sb = eR(nc.sbuf_tensor("rt_sb", [128, 8], F32))
                rs_sb = eR(nc.sbuf_tensor("rs_sb", [128, 8], F32))
                w_sb = eR(nc.sbuf_tensor("w_sb", [128, 8], F32))
                rstat = eR(nc.sbuf_tensor("rstat", [128, 4], F32))
                pR = eR(nc.psum_tensor("pR", [8, 512], F32))
                pRT = eR(nc.psum_tensor("pRT", [128, 8], F32))
                pWT = eR(nc.psum_tensor("pWT", [8, 128], F32))

                s_rr = sS.n
                t_rt_last = tS.n
                W(tn, vS, v_ln_done)
                W(tn, dG, d_pre2)
                W(tn, gS, g_wa)
                for m in range(NCH):
                    W(tn, sS, s_rr)
                    for ci in range(CT):
                        ins = tn.matmul(pR[:], rwr_sb[:, ci, :],
                                        xnT[:, ci, 512 * m:512 * (m + 1)],
                                        start=(ci == 0), stop=(ci == CT - 1))
                    t_rm = tS.inc(ins)
                    W(sl, tS, t_rm)
                    W(sl, tS, t_rt_last)
                    s_rr = ss.activation(out=rre[:], in_=pR[:], func=AF.Identity,
                                         bias=rbr_sb[:], scale=1.0)
                    for k in range(4):
                        stt = 4 * m + k
                        W(tn, sS, s_rr)
                        ins = tn.transpose(pRT[:], rre[:, 128 * k:128 * (k + 1)],
                                           identf[0:8, 0:8])
                        t_rt_last = tS.inc(ins)
                        W(ve, tS, t_rt_last)
                        vv.tensor_copy(out=rt_sb[:], in_=pRT[:])
                        vv.reduce_max(out=rstat[:, 0:1], in_=rt_sb[:], axis=AX.X)
                        v_rm = vv.tensor_scalar(out=rstat[:, 1:2], in0=rstat[:, 0:1],
                                                scalar1=-1.0, scalar2=None,
                                                op0=ALU.mult)
                        W(sl, vS, v_rm)
                        s_re = ss.activation(out=rs_sb[:], in_=rt_sb[:],
                                             func=AF.Exp, bias=rstat[:, 1:2],
                                             scale=1.0)
                        W(ve, sS, s_re)
                        vv.reduce_sum(out=rstat[:, 2:3], in_=rs_sb[:], axis=AX.X)
                        vv.reciprocal(out=rstat[:, 3:4], in_=rstat[:, 2:3])
                        vv.tensor_scalar(out=rs_sb[:], in0=rs_sb[:],
                                         scalar1=rstat[:, 3:4], scalar2=None,
                                         op0=ALU.mult)
                        vv.reduce_max(out=rstat[:, 0:1], in_=rs_sb[:], axis=AX.X)
                        vv.tensor_scalar(out=rt_sb[:], in0=rs_sb[:],
                                         scalar1=rstat[:, 0:1], scalar2=None,
                                         op0=ALU.is_equal)
                        vv.tensor_tensor(out=w_sb[:], in0=rs_sb[:], in1=rt_sb[:],
                                         op=ALU.mult)
                        vv.tensor_tensor(out=w_sb[:], in0=rs_sb[:], in1=w_sb[:],
                                         op=ALU.subtract)
                        vv.reduce_max(out=rstat[:, 1:2], in_=w_sb[:], axis=AX.X)
                        vv.tensor_scalar(out=w_sb[:], in0=w_sb[:],
                                         scalar1=rstat[:, 1:2], scalar2=None,
                                         op0=ALU.is_equal)
                        vv.tensor_tensor(out=rt_sb[:], in0=rt_sb[:], in1=w_sb[:],
                                         op=ALU.add)
                        v_w = vv.tensor_tensor(out=w_sb[:], in0=rs_sb[:],
                                               in1=rt_sb[:], op=ALU.mult)
                        W(tn, vS, v_w)
                        ins = tn.transpose(pWT[:], w_sb[:], identf[:])
                        t_wt = tS.inc(ins)
                        W(ve, tS, t_wt)
                        vv.tensor_copy(out=W_aug[0:8, 128 * stt:128 * (stt + 1)],
                                       in_=pWT[:])
                v_W_done = vS.n
            if dbg:
                W(sy, vS, v_W_done)
                W(sy, dDBG, dDBG.n)
                dma(sy, dDBG, taps["d_W"][:], W_aug[:])
            if phase_limit < 2:
                wait_all_dma(sy)
                wait_all_dma(gp)
                return nc

            with ExitStack() as stMM:
                eMM = stMM.enter_context
                QT = [eMM(nc.sbuf_tensor(f"QT{h}", [64, T], BF16)) for h in range(2)]
                KT = [eMM(nc.sbuf_tensor(f"KT{h}", [64, T], BF16)) for h in range(2)]
                VT = eMM(nc.sbuf_tensor("VT", [128, T], BF16))
                Vaug = eMM(nc.sbuf_tensor("Vaug", [128, TT, 66], BF16))
                P2 = eMM(nc.sbuf_tensor("P2", [128, 2, 1024], BF16))
                wbuf = eMM(nc.sbuf_tensor("wbuf", [128, 2, CT, 128], BF16))
                recro = eMM(nc.sbuf_tensor("recro", [128, 512], F32))
                stA_ = eMM(nc.sbuf_tensor("stA", [128, 16], F32))
                csml = eMM(nc.sbuf_tensor("csml", [64, 2, 512], BF16))
                cunb = eMM(nc.sbuf_tensor("cunb", [65, 2, 512], F32))
                acc = eMM(nc.sbuf_tensor("acc", [128, DT, FW], F32))
                H = eMM(nc.sbuf_tensor("H", [128, 2, FT, FW], BF16))
                rw1r = eMM(nc.sbuf_tensor("rw1r", [128, 2, 2, CT, 128], BF16))
                rw2r = eMM(nc.sbuf_tensor("rw2r", [128, 2, FT, 128], BF16))
                bcw = eMM(nc.sbuf_tensor("bcw", [128, 2, FW], BF16))
                pS = [eMM(nc.psum_tensor(f"pS{i}", [128, 1024], F32)) for i in range(2)]
                pC = eMM(nc.psum_tensor("pC", [65, 512], F32))
                pH = [eMM(nc.psum_tensor("pH0", [128, 512], F32))]
                pSM = eMM(nc.psum_tensor("pSM", [128, 512], F32))
                pVb = eMM(nc.psum_tensor("pVb", [128, 512], BF16))

                W(gp, gS, g_pre)
                W(gp, vS, vS.n)
                W(gp, sS, sS.n)
                W(gp, tS, tS.n)
                wait_all_dma(gp)
                g_rows = gg.memset(Vaug[:, :, 64:65], 1.0)

                precast = [w for i in range(E9)
                           for w in ((rw1_d[i], rw1b_d[i]), (rw2_d[i], rw2b_d[i]))]
                d_wcast = [0]

                fs = {
                    "v_pH": [0], "t_u_used": {}, "d_u": {},
                    "t_r2_used": {}, "d_r2": {}, "v_hc": {},
                    "s_gel": {}, "v_wm": {}, "t_sm": {},
                    "v_accdone": {}, "d_spill": {0: 0},
                    "v_evSM": 0, "s_evSM": 0, "s_bcw": {},
                    "nui": 2, "nsmi": 2,
                }
                NU = FT // 2

                def issue_u(ui):
                    g, r = divmod(ui, NU)
                    c, ei = divmod(g, E9)
                    if c >= FCH:
                        return
                    if ui - 2 in fs["t_u_used"]:
                        W(gp, tS, fs["t_u_used"][ui - 2])
                    W(gp, dPC, d_wcast[0])
                    fs["d_u"][ui] = dma(
                        gp, dW1[ui % 2], rw1r[:, ui % 2, :, :, :],
                        rw1b_d[ei][:, 2 * r:2 * r + 2, :, :])

                def issue_r2(smi):
                    g, d = divmod(smi, DT)
                    c, ei = divmod(g, E9)
                    if c >= FCH:
                        return
                    if smi - 2 in fs["t_r2_used"]:
                        W(gp, tS, fs["t_r2_used"][smi - 2])
                    W(gp, dPC, d_wcast[0])
                    fs["d_r2"][smi] = dma(
                        gp, dR2[smi % 2], rw2r[:, smi % 2, :, :],
                        rw2b_d[ei][:, d, :, :])

                def blk_bcw(c, ei):
                    g = c * E9 + ei
                    W(tn, vS, fs["v_wm"].get(g - 2, 0))
                    W(tn, vS, fs["v_evSM"])
                    W(tn, sS, fs["s_evSM"])
                    ins = tn.matmul(pSM[:], sel_sb[:, 128 * ei:128 * (ei + 1)],
                                    W_aug[:, FW * c:FW * (c + 1)],
                                    start=True, stop=True)
                    t_ = tS.inc(ins)
                    W(sl, tS, t_)
                    fs["s_bcw"][g] = ss.activation(
                        out=bcw[:, g % 2, :], in_=pSM[:], func=AF.Identity,
                        scale=1.0)
                    fs["s_evSM"] = fs["s_bcw"][g]

                def blk_fm(c, ei, fj):
                    g = c * E9 + ei
                    ui = g * NU + fj // 2
                    W(tn, dW1[ui % 2], fs["d_u"][ui])
                    W(tn, vS, fs["v_pH"][0])
                    for ci in range(CT):
                        ins = tn.matmul(pH[0][:],
                                        rw1r[:, ui % 2, fj % 2, ci, :],
                                        xnT[:, ci, FW * c:FW * (c + 1)],
                                        start=(ci == 0), stop=(ci == CT - 1))
                    t_fm = tS.inc(ins)
                    fs["t_u_used"][ui] = t_fm
                    if fj % 2 == 1 and fs["nui"] == ui + 2:
                        issue_u(fs["nui"])
                        fs["nui"] += 1
                    W(ve, tS, t_fm)
                    if g - 2 in fs["t_sm"]:
                        W(ve, tS, fs["t_sm"][g - 2])
                    fs["v_hc"][(g, fj)] = vv.tensor_copy(
                        out=H[:, g % 2, fj, :], in_=pH[0][:])
                    fs["v_pH"][0] = fs["v_hc"][(g, fj)]

                def flush_gelus(pend):
                    for g in pend:
                        c, ei = divmod(g, E9)
                        last_v = 0
                        for fj in range(FT):
                            W(sl, vS, fs["v_hc"][(g, fj)])
                            last_s = ss.activation(out=H[:, g % 2, fj, :],
                                                   in_=H[:, g % 2, fj, :],
                                                   func=AF.Gelu,
                                                   bias=rb1_sb[:, ei, fj:fj + 1],
                                                   scale=1.0)
                            if ei < 8:
                                W(ve, sS, last_s)
                                last_v = vv.tensor_tensor(
                                    out=H[:, g % 2, fj, :],
                                    in0=H[:, g % 2, fj, :],
                                    in1=bcw[:, g % 2, :], op=ALU.mult)
                        fs["s_gel"][g] = last_s
                        fs["v_wm"][g] = last_v

                def blk_sm(c, ei, d):
                    g = c * E9 + ei
                    smi = g * DT + d
                    W(tn, dR2[smi % 2], fs["d_r2"][smi])
                    W(tn, sS, fs["s_gel"][g])
                    W(tn, vS, fs["v_wm"][g])
                    W(tn, vS, fs["v_evSM"])
                    W(tn, sS, fs["s_evSM"])
                    for fj in range(FT):
                        ins = tn.matmul(pSM[:], rw2r[:, smi % 2, fj, :],
                                        H[:, g % 2, fj, :],
                                        start=(fj == 0), stop=(fj == FT - 1))
                    t_sm = tS.inc(ins)
                    fs["t_r2_used"][smi] = t_sm
                    if d == DT - 1:
                        fs["t_sm"][g] = t_sm
                    if fs["nsmi"] == smi + 2:
                        issue_r2(fs["nsmi"])
                        fs["nsmi"] += 1
                    W(ve, tS, t_sm)
                    if ei == 0:
                        W(ve, dSP, fs["d_spill"].get(c, 0))
                        fs["v_evSM"] = vv.tensor_copy(out=acc[:, d, :], in_=pSM[:])
                    else:
                        fs["v_evSM"] = vv.tensor_tensor(
                            out=acc[:, d, :], in0=acc[:, d, :], in1=pSM[:],
                            op=ALU.add)

                def blk_rb2s(c, d):
                    W(tn, vS, fs["v_evSM"])
                    W(tn, sS, fs["s_evSM"])
                    ins = tn.matmul(pSM[:], rb2_sb[:, 128 * d:128 * (d + 1)],
                                    W_aug[:, FW * c:FW * (c + 1)],
                                    start=True, stop=True)
                    t_ = tS.inc(ins)
                    W(ve, tS, t_)
                    fs["v_evSM"] = vv.tensor_tensor(
                        out=acc[:, d, :], in0=acc[:, d, :], in1=pSM[:], op=ALU.add)
                    if d == DT - 1:
                        fs["v_accdone"][c] = fs["v_evSM"]

                def blk_spill(c):
                    W(sy, vS, fs["v_accdone"][c])
                    for d in range(DT):
                        dma(sy, dSP, ffn_d[d][:, FW * c:FW * (c + 1)], acc[:, d, :])
                    fs["d_spill"][c + 1] = dSP.n

                fm_lane = []
                sm_lane = []
                for c in range(FCH):
                    for ei in range(E9):
                        if ei < 8:
                            fm_lane.append(("bcw", c, ei))
                        for fj in range(FT):
                            fm_lane.append(("fm", c, ei, fj))
                        for d in range(DT):
                            sm_lane.append(("sm", c, ei, d))
                    for d in range(DT):
                        sm_lane.append(("rb2s", c, d))
                    sm_lane.append(("spill", c))
                total_blocks = len(fm_lane) + len(sm_lane)
                cur = {"fm": 0, "sm": 0, "emitted": 0}
                pending_gelu = []

                def sm_ready(b):
                    if b[0] == "sm":
                        return (b[1] * E9 + b[2]) in fs["s_gel"]
                    if b[0] == "rb2s":
                        return (b[1] * E9 + E9 - 1) in fs["s_gel"]
                    if b[0] == "spill":
                        return b[1] in fs["v_accdone"]
                    return True

                def fm_ready(b):
                    g = b[1] * E9 + b[2]
                    if b[0] == "fm" and b[3] == 0:
                        if g >= 2 and (g - 2) not in fs["t_sm"]:
                            return False
                    if b[0] == "bcw":
                        if g >= 2 and (g - 2) not in fs["v_wm"]:
                            return False
                    return True

                def run_block(b):
                    kind = b[0]
                    if kind == "bcw":
                        blk_bcw(b[1], b[2])
                    elif kind == "fm":
                        blk_fm(b[1], b[2], b[3])
                        if b[3] == FT - 1:
                            pending_gelu.append(b[1] * E9 + b[2])
                    elif kind == "sm":
                        blk_sm(b[1], b[2], b[3])
                    elif kind == "rb2s":
                        blk_rb2s(b[1], b[2])
                    elif kind == "spill":
                        blk_spill(b[1])
                    cur["emitted"] += 1

                def pump(target):
                    progress = True
                    while cur["emitted"] < target and progress:
                        progress = False
                        while cur["sm"] < len(sm_lane) and cur["emitted"] < target:
                            b = sm_lane[cur["sm"]]
                            if not sm_ready(b):
                                break
                            run_block(b)
                            cur["sm"] += 1
                            progress = True
                        while cur["fm"] < len(fm_lane) and cur["emitted"] < target:
                            b = fm_lane[cur["fm"]]
                            if not fm_ready(b):
                                break
                            run_block(b)
                            cur["fm"] += 1
                            progress = True

                t_projdone = [0] * NP
                prev_exp = None
                prev_tPV = None
                v_ctx_prev = 0
                d_cs = [0, 0]
                pending_norm = []

                def do_norm(item):
                    h_idx, q_, v_rec_ = item
                    sl_ = q_ % 2
                    W(tn, vS, v_rec_)
                    W(tn, vS, fs["v_evSM"])
                    W(tn, sS, fs["s_evSM"])
                    ins_ = tn.matmul(pSM[0:64, :],
                                     ones_all[64 * sl_:64 * sl_ + 1, 0:64],
                                     recro[64 * sl_:64 * sl_ + 1, :],
                                     start=True, stop=True)
                    t_bc = tS.inc(ins_)
                    W(ve, tS, t_bc)
                    vv.tensor_copy(out=recro[0:64, :], in_=pSM[0:64, :])
                    fs["v_evSM"] = vS.n
                    W(ve, dR[sl_], d_cs[sl_])
                    v_cm = vv.tensor_tensor(
                        out=csml[:, sl_, :],
                        in0=cunb[0:64, sl_, :], in1=recro[0:64, :],
                        op=ALU.mult)
                    W(sy, vS, v_cm)
                    d_cs[sl_] = dma(sy, dR[sl_],
                                    ctx_d[h_idx][:, 512 * q_:512 * (q_ + 1)],
                                    csml[:, sl_, :])
                n_attn_chunks = NP * 2 * NCH
                chunk_no = 0
                total_kts = n_attn_chunks * 32
                kt_no = [0]
                kt_start = 16 * 32

                def _tgt():
                    k = kt_no[0]
                    if not d_wcast[0] or k < kt_start:
                        return 0
                    return total_blocks * (k - kt_start) // (total_kts - kt_start)

                def micro_pump():
                    kt_no[0] += 1
                    if kt_no[0] % 2 == 0:
                        pump(min(_tgt(), cur["emitted"] + 2))

                def fill(n=2):
                    pump(min(_tgt(), cur["emitted"] + n))

                for p in range(NP):
                    h0, h1 = 2 * p, 2 * p + 1
                    if p >= 1:
                        W(gp, tS, t_projdone[p - 1])
                    dma(gp, dG, wbuf[:, 0, :, :], wq_d[p])
                    dma(gp, dG, wbuf[:, 1, :, :], wk_d[p])
                    d_wqk = dG.n
                    W(ve, dG, d_wqk)
                    W(ve, vS, v_pre)
                    vw0 = vv.tensor_scalar(out=wbuf[:, 0, :, :], in0=wbuf[:, 0, :, :],
                                           scalar1=scol[:, 0:1], scalar2=None,
                                           op0=ALU.mult)
                    vw1 = vv.tensor_scalar(out=wbuf[:, 1, :, :], in0=wbuf[:, 1, :, :],
                                           scalar1=scol[:, 1:2], scalar2=None,
                                           op0=ALU.mult)

                    slots = [(pS[0], 0), (pS[0], 512), (pS[1], 0), (pS[1], 512)]
                    slot_tick = [sS.n] * 4
                    kch = 0

                    def proj(mat, lo_dst, hi_dst, b_sb, v_w, hi_wait=0):
                        nonlocal kch
                        first = tS.n
                        W(tn, vS, v_w)
                        W(tn, vS, v_ln_done)
                        W(tn, gS, g_rows)
                        for m in range(NCH):
                            ps, off = slots[kch % 4]
                            W(tn, sS, slot_tick[kch % 4])
                            for ci in range(CT):
                                ins = tn.matmul(
                                    ps[:, off:off + 512], wbuf[:, mat, ci, :],
                                    xnT[:, ci, 512 * m:512 * (m + 1)],
                                    start=(ci == 0), stop=(ci == CT - 1))
                            t_mm = tS.inc(ins)
                            if m == 0:
                                first = t_mm
                            W(sl, tS, t_mm)
                            if hi_wait:
                                W(sl, dRB, hi_wait)
                            ss.activation(out=lo_dst[0:64, 512 * m:512 * (m + 1)],
                                          in_=ps[0:64, off:off + 512],
                                          func=AF.Identity,
                                          bias=b_sb[0:64, h0:h0 + 1], scale=1.0)
                            slot_tick[kch % 4] = ss.activation(
                                out=hi_dst[64:128, 512 * m:512 * (m + 1)],
                                in_=ps[64:128, off:off + 512], func=AF.Identity,
                                bias=b_sb[64:128, h1:h1 + 1], scale=1.0)
                            kch += 1
                        return slot_tick[(kch - 1) % 4], first

                    s_q, t_q0 = proj(0, QT[0], VT, bq_sb, vw0)
                    W(gp, sS, s_q)
                    d_q1 = dma(gp, dRB, QT[1][:], VT[64:128, :])
                    W(gp, tS, tS.n)
                    dma(gp, dG, wbuf[:, 0, :, :], wv_d[p])
                    d_wv = dG.n
                    W(ve, dG, d_wv)
                    vwv = vv.tensor_scalar(out=wbuf[:, 0, :, :], in0=wbuf[:, 0, :, :],
                                           scalar1=scol[:, 2:3], scalar2=None,
                                           op0=ALU.mult)
                    s_k, _ = proj(1, KT[0], VT, bk_sb, vw1, hi_wait=d_q1)
                    W(gp, sS, s_k)
                    d_k1 = dma(gp, dRB, KT[1][:], VT[64:128, :])
                    s_v, _ = proj(0, VT, VT, bv_sb, vwv, hi_wait=d_k1)
                    t_projdone[p] = tS.n
                    if p == 0:
                        for src, dst in precast:
                            dma(gp, dPC, dst, src)
                        precast = []
                        d_wcast[0] = dPC.n
                        issue_u(0)
                        issue_u(1)
                        issue_r2(0)
                        issue_r2(1)

                    for hh in range(2):
                        qt, kt = QT[hh], KT[hh]
                        idsl = ident[0:64, 0:64] if hh == 0 else \
                            ident[64:128, 64:128]
                        v_vcp = vS.n
                        W(tn, sS, s_v)
                        for gvt in range(4):
                            fill()
                            W(tn, vS, v_vcp)
                            for j in range(8):
                                kti = 8 * gvt + j
                                ins = tn.transpose(
                                    pVb[:, 64 * j:64 * (j + 1)],
                                    VT[64 * hh:64 * (hh + 1),
                                       128 * kti:128 * (kti + 1)],
                                    idsl)
                            t_v = tS.inc(ins)
                            W(ve, tS, t_v)
                            v_vcp = vv.tensor_copy(
                                out=Vaug[:, 8 * gvt:8 * (gvt + 1), 0:64],
                                in_=pVb[:].rearrange("p (a b) -> p a b", a=8))
                        v_vaug = v_vcp

                        if hh == 1:
                            W(ve, dRB, d_k1)
                        for which, tt in ((0, qt), (1, kt)):
                            W(ve, sS, s_q if which == 0 else s_k)
                            v_am = vv.reduce_max(out=stA_[0:64, 0:1], in_=tt[:],
                                                 axis=AX.X,
                                                 apply_absolute_value=True)
                            fill()
                            W(tn, vS, v_am)
                            W(tn, vS, v_ctx_prev)
                            ins = tn.transpose(pC[0:1, 0:64], stA_[0:64, 0:1],
                                               identf[0:64, 0:64])
                            t_am = tS.inc(ins)
                            W(ve, tS, t_am)
                            vv.reduce_max(out=stA_[0:1, 8 + which:9 + which],
                                          in_=pC[0:1, 0:64], axis=AX.X)
                            fill()
                        vv.tensor_tensor(out=stA_[0:1, 10:11], in0=stA_[0:1, 8:9],
                                         in1=stA_[0:1, 9:10], op=ALU.mult)
                        v_negc = vv.tensor_scalar(out=stA_[0:1, 13:14],
                                                  in0=stA_[0:1, 10:11],
                                                  scalar1=-64.0, scalar2=None,
                                                  op0=ALU.mult)
                        W(tn, vS, v_negc)
                        W(tn, sS, sS.n)
                        ins = tn.matmul(pS[0][:, 0:1], ones_r128[:],
                                        stA_[0:1, 13:14], start=True, stop=True)
                        t_nc = tS.inc(ins)
                        W(ve, tS, t_nc)
                        W(ve, sS, sS.n)
                        v_ncol = vv.tensor_copy(out=stA_[:, 15:16],
                                                in_=pS[0][:, 0:1])

                        W(tn, vS, v_ncol)
                        W(tn, vS, v_vaug)
                        for q in range(NCH):
                            s_exp = [0] * 16
                            t_PV = [0] * 16
                            for j in range(16):
                                if j >= 2:
                                    W(tn, sS, s_exp[j - 2])
                                elif prev_exp is not None:
                                    W(tn, sS, prev_exp[14 + j])
                                for half in range(2):
                                    kt_i = 2 * j + half
                                    ins = tn.matmul(
                                        pS[j % 2][:, 512 * half:512 * (half + 1)],
                                        kt[:, 128 * kt_i:128 * (kt_i + 1)],
                                        qt[:, 512 * q:512 * (q + 1)],
                                        start=True, stop=True)
                                t_S = tS.inc(ins)
                                if j >= 1:
                                    jj = j - 1
                                    W(tn, sS, s_exp[jj])
                                    if jj == 0:
                                        W(tn, vS, v_ctx_prev)
                                    for half in range(2):
                                        kt_i = 2 * jj + half
                                        ins = tn.matmul(
                                            pC[:], Vaug[:, kt_i, 0:65],
                                            P2[:, jj % 2,
                                               512 * half:512 * (half + 1)],
                                            start=(kt_i == 0), stop=False)
                                    t_PV[jj] = tS.inc(ins)
                                W(sl, tS, t_S)
                                if j >= 2:
                                    W(sl, tS, t_PV[j - 2])
                                elif prev_tPV is not None:
                                    W(sl, tS, prev_tPV[14 + j])
                                s_exp[j] = ss.activation(
                                    out=P2[:, j % 2, :], in_=pS[j % 2][:],
                                    func=AF.Exp, bias=stA_[:, 15:16], scale=1.0)
                                if j == 4 and pending_norm:
                                    do_norm(pending_norm.pop(0))
                                micro_pump()
                                micro_pump()
                            W(tn, sS, s_exp[15])
                            for half in range(2):
                                ins = tn.matmul(pC[:], Vaug[:, 30 + half, 0:65],
                                                P2[:, 1,
                                                   512 * half:512 * (half + 1)],
                                                start=False, stop=(half == 1))
                            t_pvl = tS.inc(ins)
                            W(ve, tS, t_pvl)
                            v_ctx_prev = vv.tensor_copy(out=cunb[:, q % 2, :],
                                                        in_=pC[:])
                            v_rec = vv.reciprocal(out=recro[64 * (q % 2):
                                                            64 * (q % 2) + 1, :],
                                                  in_=cunb[64:65, q % 2, :])
                            pending_norm.append((2 * p + hh, q, v_rec))
                            prev_exp = s_exp
                            prev_tPV = t_PV
                            chunk_no += 1
                            if d_wcast[0] and pending_gelu:
                                flush_gelus(pending_gelu)
                                pending_gelu.clear()

                while pending_norm:
                    do_norm(pending_norm.pop(0))
                guard = 0
                while cur["emitted"] < total_blocks:
                    if pending_gelu:
                        flush_gelus(pending_gelu)
                        pending_gelu.clear()
                    before = cur["emitted"]
                    pump(total_blocks)
                    if cur["emitted"] == before and not pending_gelu:
                        raise RuntimeError("ffn scheduler deadlock")
                    guard += 1
                    if guard > 100000:
                        raise RuntimeError("ffn scheduler livelock")
                d_cs_done = (dR[0].n, dR[1].n)
                d_sp_done = dSP.n
                t_merged = tS.n
                v_merged = vS.n
                s_merged = sS.n

        if phase_limit < 3:
            wait_all_dma(sy)
            wait_all_dma(gp)
            return nc

        with ExitStack() as stF:
            eF = stF.enter_context
            ctx_sb = eF(nc.sbuf_tensor("ctx_sb", [128, CT, T], BF16))
            wo_sb = eF(nc.sbuf_tensor("wo_sb", [128, CT, DT, 128], BF16))
            ffn_sb = eF(nc.sbuf_tensor("ffn_sb", [128, DT, 1024], F32))
            xt2c = eF(nc.sbuf_tensor("xt2c", [128, 2, D], F32))
            ostg = eF(nc.sbuf_tensor("ostg", [128, 2, D], F32))
            pO = [eF(nc.psum_tensor(f"pO{i}", [128, 512], F32)) for i in range(2)]
            pD = [eF(nc.psum_tensor(f"pD{i}", [128, 512], F32)) for i in range(2)]

            W(gp, tS, t_merged)
            W(gp, vS, v_merged)
            W(gp, sS, s_merged)
            wait_all_dma(gp)
            dma(gp, dG, wo_sb[:], wo_d[:])
            d_wo = dG.n
            W(ve, dG, d_wo)
            v_wo = vv.tensor_scalar(
                out=wo_sb[:].rearrange("p a b c -> p (a b c)"),
                in0=wo_sb[:].rearrange("p a b c -> p (a b c)"),
                scalar1=scol[:, 3:4], scalar2=None, op0=ALU.mult)
            W(sy, tS, t_merged)
            W(sy, vS, v_merged)
            W(sy, sS, s_merged)
            W(sy, dR[0], d_cs_done[0])
            W(sy, dR[1], d_cs_done[1])
            for h in range(16):
                dma(sy, dCX, ctx_sb[64 * (h % 2):64 * (h % 2) + 64, h // 2, :],
                    ctx_d[h])
            d_ctx_l = dCX.n

            v_pDt = [0, 0]
            v_addfull = [vS.n] * 2
            d_og = [0, 0]
            t_pD_done = 0
            for c in range(4):
                W(sy, dSP, d_sp_done)
                W(sy, tS, t_pD_done)
                for d in range(DT):
                    dma(sy, dFB, ffn_sb[:, d, :],
                        ffn_d[d][:, 1024 * c:1024 * (c + 1)])
                d_fl = dFB.n
                v_ev = [vS.n] * 2
                W(tn, vS, v_wo)
                W(tn, dCX, d_ctx_l)
                for d in range(DT):
                    for m in range(2):
                        W(tn, vS, v_ev[m])
                        for ci in range(CT):
                            ins = tn.matmul(
                                pO[m][:], wo_sb[:, ci, d, :],
                                ctx_sb[:, ci, 1024 * c + 512 * m:
                                       1024 * c + 512 * (m + 1)],
                                start=(ci == 0), stop=(ci == CT - 1))
                        t_mm = tS.inc(ins)
                        W(ve, tS, t_mm)
                        W(ve, dFB, d_fl)
                        v_ev[m] = vv.scalar_tensor_tensor(
                            out=ffn_sb[:, d, 512 * m:512 * (m + 1)],
                            in0=pO[m][:], scalar=bo_sb[:, d:d + 1],
                            in1=ffn_sb[:, d, 512 * m:512 * (m + 1)],
                            op0=ALU.add, op1=ALU.add)
                v_comb = max(v_ev)
                for stt in range(8):
                    buf = stt % 2
                    row = 128 * (8 * c + stt)
                    W(sy, vS, v_addfull[buf])
                    d_xt = dma(sy, dX[buf], xt2c[:, buf, :], x_d[row:row + 128, :])
                    for r in range(2):
                        W(tn, vS, v_pDt[r])
                        W(tn, vS, v_comb)
                        for dd in range(4):
                            d = 4 * r + dd
                            ins = tn.transpose(
                                pD[r][:, 128 * dd:128 * (dd + 1)],
                                ffn_sb[:, d, 128 * stt:128 * (stt + 1)],
                                identf[:])
                        t_tp = tS.inc(ins)
                        t_pD_done = t_tp
                        W(ve, tS, t_tp)
                        W(ve, dX[buf], d_xt)
                        W(ve, dOut[buf], d_og[buf])
                        v_pDt[r] = vv.tensor_tensor(
                            out=ostg[:, buf, 512 * r:512 * (r + 1)],
                            in0=xt2c[:, buf, 512 * r:512 * (r + 1)],
                            in1=pD[r][:], op=ALU.add)
                    v_addfull[buf] = vS.n
                    W(sy, vS, v_addfull[buf])
                    d_og[buf] = dma(sy, dOut[buf], out_d[row:row + 128, :],
                                    ostg[:, buf, :])

        wait_all_dma(sy)
        wait_all_dma(gp)
    return nc


def _host_layouts(inp):
    f = np.float32

    def a(k):
        return np.asarray(inp[k], dtype=f)

    wq = a("L0_attn_w_q"); wk = a("L0_attn_w_k")
    wv = a("L0_attn_w_v"); wo = a("L0_attn_w_o")

    def wtile(w):
        return np.ascontiguousarray(
            w.T.reshape(8, 128, 8, 128).transpose(2, 1, 0, 3))

    def btile(b):
        t = b.reshape(16, 64).T
        return np.ascontiguousarray(np.concatenate([t, t], axis=0))

    rw1 = a("L0_ffn_rw1"); sw1 = a("L0_ffn_sw1")
    rw2 = a("L0_ffn_rw2"); sw2 = a("L0_ffn_sw2")
    rb1 = a("L0_ffn_rb1"); sb1 = a("L0_ffn_sb1")
    rb2 = a("L0_ffn_rb2"); sb2 = a("L0_ffn_sb2")
    rw1s = np.concatenate([rw1, sw1[None]], axis=0)
    rw2s = np.concatenate([rw2, sw2[None]], axis=0)
    rb1s = np.concatenate([rb1, sb1[None]], axis=0)
    rb2s = np.concatenate([rb2, sb2[None]], axis=0)

    shared = {
        "wq_t": wtile(wq), "wk_t": wtile(wk), "wv_t": wtile(wv),
        "wo_t": np.ascontiguousarray(
            wo.T.reshape(8, 128, 8, 128).transpose(1, 0, 2, 3)),
        "bq_t": btile(a("L0_attn_b_q")), "bk_t": btile(a("L0_attn_b_k")),
        "bv_t": btile(a("L0_attn_b_v")),
        "bo_t": np.ascontiguousarray(a("L0_attn_b_o").reshape(8, 128).T),
        "s4": np.stack([a("L0_attn_s_q").reshape(()), a("L0_attn_s_k").reshape(()),
                        a("L0_attn_s_v").reshape(()), a("L0_attn_s_o").reshape(())]
                       ).reshape(1, 4).astype(f),
        "rw1_t": np.ascontiguousarray(
            rw1s.reshape(9, 8, 128, 16, 128).transpose(0, 2, 3, 1, 4)),
        "rw2_t": np.ascontiguousarray(
            rw2s.reshape(9, 16, 128, 8, 128).transpose(0, 2, 3, 1, 4)),
        "rb1_t": np.ascontiguousarray(rb1s.reshape(9, 16, 128).transpose(2, 0, 1)),
        "rb2s": np.ascontiguousarray(rb2s),
        "rwr_t": np.ascontiguousarray(
            a("L0_ffn_router_w").reshape(8, 128, 8).transpose(1, 0, 2)),
        "rbr_t": np.ascontiguousarray(a("L0_ffn_router_b").reshape(8, 1)),
        "sel9": np.ascontiguousarray(
            np.repeat(np.eye(9, 8, dtype=f), 128, axis=1)),
    }
    x = np.asarray(inp["L0_attn_x"], dtype=f)
    return shared, x


def kernel(**inp):
    shared, x = _host_layouts(inp)
    nc = build_program(dbg=_DEBUG_TAPS)
    in_maps = []
    for b in range(8):
        m = dict(shared)
        m["x"] = np.ascontiguousarray(x[b])
        in_maps.append(m)
    res = run_bass_kernel_spmd(nc, in_maps, core_ids=list(range(8)))
    out = np.stack([np.asarray(r["out"], dtype=np.float32) for r in res.results],
                   axis=0)
    if _DEBUG_TAPS:
        kernel._last_results = res
    return out


# version 11
# speedup vs baseline: 1.0206x; 1.0206x over previous
import numpy as np
from contextlib import ExitStack

import concourse.bass as bass
import concourse.mybir as mybir
from concourse.bass_utils import run_bass_kernel_spmd

F32 = mybir.dt.float32
BF16 = mybir.dt.bfloat16
AF = mybir.ActivationFunctionType
ALU = mybir.AluOpType
AX = mybir.AxisListType

T = 4096
D = 1024
NP = 8
CT = 8
FT = 16
DT = 8
E9 = 9
NCH = 8
TT = 32
FCH = 8
FW = 512

_DEBUG_TAPS = False


class _Sem:
    def __init__(self, st, nc, name, step=1):
        self.h = st.enter_context(nc.semaphore(name))
        self.n = 0
        self.step = step

    def inc(self, instr):
        instr.then_inc(self.h, self.step)
        self.n += self.step
        return self.n


class _Waiter:
    def __init__(self):
        self.seen = {}

    def __call__(self, eng, sem, val=None):
        v = sem.n if val is None else val
        if v <= 0:
            return
        k = (id(eng), id(sem))
        if self.seen.get(k, 0) < v:
            eng.wait_ge(sem.h, v)
            self.seen[k] = v


class _Auto:

    def __init__(self, eng, sem, waiter, skip=("wait_ge", "dma_start")):
        self._e = eng
        self._s = sem
        self._W = waiter
        self._skip = set(skip)

    def __getattr__(self, name):
        f = getattr(self._e, name)
        if name in self._skip or name.startswith("_"):
            return f

        def g(*a, **k):
            self._W(self._e, self._s, self._s.n)
            ins = f(*a, **k)
            self._s.n += 1
            ins.then_inc(self._s.h, 1)
            return self._s.n

        return g


def build_program(dbg=False, phase_limit=99):
    nc = bass.Bass("TRN2", target_bir_lowering=False, debug=False)
    tn, sl, ve, gp, sy = nc.tensor, nc.scalar, nc.vector, nc.gpsimd, nc.sync

    x_d = nc.dram_tensor("x", [T, D], F32, kind="ExternalInput")
    wq_d = nc.dram_tensor("wq_t", [NP, 128, CT, 128], F32, kind="ExternalInput")
    wk_d = nc.dram_tensor("wk_t", [NP, 128, CT, 128], F32, kind="ExternalInput")
    wv_d = nc.dram_tensor("wv_t", [NP, 128, CT, 128], F32, kind="ExternalInput")
    wo_d = nc.dram_tensor("wo_t", [128, CT, DT, 128], F32, kind="ExternalInput")
    bq_d = nc.dram_tensor("bq_t", [128, 16], F32, kind="ExternalInput")
    bk_d = nc.dram_tensor("bk_t", [128, 16], F32, kind="ExternalInput")
    bv_d = nc.dram_tensor("bv_t", [128, 16], F32, kind="ExternalInput")
    bo_d = nc.dram_tensor("bo_t", [128, 8], F32, kind="ExternalInput")
    s4_d = nc.dram_tensor("s4", [1, 4], F32, kind="ExternalInput")
    rw1_d = nc.dram_tensor("rw1_t", [E9, 128, FT, CT, 128], F32, kind="ExternalInput")
    rw2_d = nc.dram_tensor("rw2_t", [E9, 128, DT, FT, 128], F32, kind="ExternalInput")
    rb1_d = nc.dram_tensor("rb1_t", [128, E9, FT], F32, kind="ExternalInput")
    rb2_d = nc.dram_tensor("rb2s", [E9, D], F32, kind="ExternalInput")
    rwr_d = nc.dram_tensor("rwr_t", [128, CT, 8], F32, kind="ExternalInput")
    rbr_d = nc.dram_tensor("rbr_t", [8, 1], F32, kind="ExternalInput")
    sel_d = nc.dram_tensor("sel9", [E9, 8 * 128], F32, kind="ExternalInput")
    out_d = nc.dram_tensor("out", [T, D], F32, kind="ExternalOutput")
    scrk = "ExternalOutput" if dbg else "Internal"
    ffn_d = nc.dram_tensor("ffn_scr", [DT, 128, T], F32, kind=scrk)
    ctx_d = nc.dram_tensor("ctx_scr", [16, 64, T], BF16, kind=scrk)
    rw1b_d = nc.dram_tensor("rw1b", [E9, 128, FT, CT, 128], BF16, kind="Internal")
    rw2b_d = nc.dram_tensor("rw2b", [E9, 128, DT, FT, 128], BF16, kind="Internal")
    taps = {}
    if dbg:
        taps["d_xnT"] = nc.dram_tensor("d_xnT", [128, CT, T], BF16, kind="ExternalOutput")
        taps["d_W"] = nc.dram_tensor("d_W", [E9, T], BF16, kind="ExternalOutput")

    W = _Waiter()
    with ExitStack() as st:
        e = st.enter_context
        tS = _Sem(st, nc, "tS")
        sS = _Sem(st, nc, "sS")
        vS = _Sem(st, nc, "vS")
        gS = _Sem(st, nc, "gS")
        dX = [_Sem(st, nc, f"dX{i}", 16) for i in range(2)]
        dR = [_Sem(st, nc, f"dR{i}", 16) for i in range(2)]
        dSP = _Sem(st, nc, "dSP", 16)
        dFB = _Sem(st, nc, "dFB", 16)
        dCX = _Sem(st, nc, "dCX", 16)
        dOut = [_Sem(st, nc, f"dOut{i}", 16) for i in range(2)]
        dR2 = [_Sem(st, nc, f"dR2{i}", 16) for i in range(2)]
        dDBG = _Sem(st, nc, "dDBG", 16)
        dG = _Sem(st, nc, "dG", 16)
        dRB = _Sem(st, nc, "dRB", 16)
        dW1 = [_Sem(st, nc, f"dW1{i}", 16) for i in range(2)]
        dPC = _Sem(st, nc, "dPC", 16)
        sync_sems = dX + dR + [dSP, dFB, dCX] + dOut + [dDBG]
        gp_sems = [dG, dRB, dPC] + dW1 + dR2

        def wait_all_dma(eng):
            for s_ in sync_sems + gp_sems:
                W(eng, s_, s_.n)

        def dma(q, qsem, out, in_):
            q.dma_start(out=out, in_=in_).then_inc(qsem.h, 16)
            qsem.n += 16
            return qsem.n

        vv = _Auto(ve, vS, W)
        ss = _Auto(sl, sS, W)
        gg = _Auto(gp, gS, W)

        ident = e(nc.sbuf_tensor("ident", [128, 128], BF16))
        identf = e(nc.sbuf_tensor("identf", [128, 128], F32))
        eps = e(nc.sbuf_tensor("eps", [128, 1], F32))
        ones_r128 = e(nc.sbuf_tensor("ones_r128", [1, 128], F32))
        ones_all = e(nc.sbuf_tensor("ones_all", [128, 64], F32))
        scol = e(nc.sbuf_tensor("scol", [128, 4], F32))
        s_sb = e(nc.sbuf_tensor("s_sb", [1, 4], F32))
        bq_sb = e(nc.sbuf_tensor("bq_sb", [128, 16], F32))
        bk_sb = e(nc.sbuf_tensor("bk_sb", [128, 16], F32))
        bv_sb = e(nc.sbuf_tensor("bv_sb", [128, 16], F32))
        bo_sb = e(nc.sbuf_tensor("bo_sb", [128, 8], F32))
        rwr_sb = e(nc.sbuf_tensor("rwr_sb", [128, CT, 8], BF16))
        rbr_sb = e(nc.sbuf_tensor("rbr_sb", [8, 1], F32))

        for dst, src in [(s_sb, s4_d), (bq_sb, bq_d), (bk_sb, bk_d), (bv_sb, bv_d),
                         (bo_sb, bo_d), (rbr_sb, rbr_d), (rwr_sb, rwr_d)]:
            dma(gp, dG, dst[:], src[:])
        d_pre = dG.n
        gg.memset(ident[:], 0.0)
        gg.affine_select(out=ident[:], in_=ident[:], compare_op=ALU.not_equal,
                         fill=1.0, base=0, pattern=[[-1, 128]], channel_multiplier=1)
        gg.memset(identf[:], 0.0)
        gg.affine_select(out=identf[:], in_=identf[:], compare_op=ALU.not_equal,
                         fill=1.0, base=0, pattern=[[-1, 128]], channel_multiplier=1)
        gg.memset(eps[:], 1e-5)
        gg.memset(ones_r128[:], 1.0)
        g_pre = gg.memset(ones_all[:], 1.0)

        with ExitStack() as stM:
            eM = stM.enter_context
            xnT = eM(nc.sbuf_tensor("xnT", [128, CT, T], BF16))
            W_aug = eM(nc.sbuf_tensor("W_aug", [E9, T], BF16))
            rb1_sb = eM(nc.sbuf_tensor("rb1_sb", [128, E9, FT], F32))
            rb2_sb = eM(nc.sbuf_tensor("rb2_sb", [E9, D], BF16))
            sel_sb = eM(nc.sbuf_tensor("sel_sb", [E9, 8 * 128], BF16))
            W(gp, dG, d_pre)
            dma(gp, dG, rb1_sb[:], rb1_d[:])
            dma(gp, dG, rb2_sb[:], rb2_d[:])
            dma(gp, dG, sel_sb[:], sel_d[:])
            d_pre2 = dG.n
            W(gp, gS, g_pre)
            g_wa = gg.memset(W_aug[:], 1.0)

            with ExitStack() as st0:
                e0 = st0.enter_context
                xt2 = e0(nc.sbuf_tensor("p0_x", [128, 2, D], F32))
                xnb = e0(nc.sbuf_tensor("p0_xn", [128, 2, D], BF16))
                st0_ = e0(nc.sbuf_tensor("p0_stat", [128, 8], F32))
                bns = e0(nc.sbuf_tensor("p0_bns", [128, 12], F32))
                pbf = e0(nc.psum_tensor("p0_pbf", [128, 2, 1024], BF16))
                ppre = e0(nc.psum_tensor("p0_pre", [128, 8], F32))

                W(tn, dG, d_pre)
                W(tn, gS, g_pre)
                for i in range(4):
                    ins = tn.matmul(ppre[:, i:i + 1], ones_r128[:], s_sb[0:1, i:i + 1],
                                    start=True, stop=True)
                t_pre = tS.inc(ins)
                W(ve, tS, t_pre)
                W(ve, dG, d_pre)
                vv.tensor_copy(out=scol[:], in_=ppre[:, 0:4])
                vv.tensor_scalar(out=scol[:, 0:1], in0=scol[:, 0:1], scalar1=0.125,
                                 scalar2=None, op0=ALU.mult)
                v_pre = vv.tensor_scalar(out=bq_sb[:], in0=bq_sb[:], scalar1=0.125,
                                         scalar2=None, op0=ALU.mult)

                v_xn = [0] * TT
                v_cp = [0] * TT
                d_x = [0] * TT
                d_x[0] = dma(sy, dX[0], xt2[:, 0, :], x_d[0:128, :])
                d_x[1] = dma(sy, dX[1], xt2[:, 1, :], x_d[128:256, :])
                v_stat = [0] * TT
                t_trs = [0] * TT
                for i in range(TT):
                    xt = xt2[:, i % 2, :]
                    if i >= 2:
                        W(ve, tS, t_trs[i - 2])
                        v_cp[i - 2] = vv.tensor_copy(
                            out=xnT[:, :, 128 * (i - 2):128 * (i - 1)],
                            in_=pbf[:, (i - 2) % 2, :]
                            .rearrange("p (c t) -> p c t", c=CT))
                    W(ve, dX[i % 2], d_x[i])
                    W(ve, vS, v_pre)
                    vv.bn_stats(out=bns[:, 0:6], in_=xt[:, 0:512])
                    vv.bn_stats(out=bns[:, 6:12], in_=xt[:, 512:1024])
                    vv.bn_aggr(out=st0_[:, 0:2], in_=bns[:])
                    v_st = vv.tensor_scalar(out=st0_[:, 4:5], in0=st0_[:, 0:1],
                                            scalar1=-1.0, scalar2=None,
                                            op0=ALU.mult)
                    v_stat[i] = v_st
                    W(sl, vS, v_st)
                    s_r = 0
                    ss.activation(out=st0_[:, 5:6], in_=st0_[:, 1:2], func=AF.Ln,
                                  bias=eps[:], scale=1.0)
                    s_r = ss.activation(out=st0_[:, 6:7], in_=st0_[:, 5:6],
                                        func=AF.Exp, scale=-0.5)
                    W(ve, sS, s_r)
                    v_xn[i] = vv.tensor_scalar(out=xnb[:, i % 2, :], in0=xt,
                                               scalar1=st0_[:, 4:5],
                                               scalar2=st0_[:, 6:7],
                                               op0=ALU.add, op1=ALU.mult)
                    if i + 2 < TT:
                        W(sy, vS, v_xn[i])
                        d_x[i + 2] = dma(sy, dX[i % 2], xt2[:, i % 2, :],
                                         x_d[128 * (i + 2):128 * (i + 3), :])
                    W(tn, vS, v_xn[i])
                    if i >= 2:
                        W(tn, vS, v_cp[i - 2])
                    for j in range(CT):
                        ins = tn.transpose(pbf[:, i % 2, 128 * j:128 * (j + 1)],
                                           xnb[:, i % 2, 128 * j:128 * (j + 1)],
                                           ident[:])
                    t_trs[i] = tS.inc(ins)
                for i in (TT - 2, TT - 1):
                    W(ve, tS, t_trs[i])
                    v_cp[i] = vv.tensor_copy(
                        out=xnT[:, :, 128 * i:128 * (i + 1)],
                        in_=pbf[:, i % 2, :].rearrange("p (c t) -> p c t", c=CT))
                v_ln_done = v_cp[TT - 1]

            if dbg:
                W(sy, vS, v_ln_done)
                W(sy, dDBG, dDBG.n)
                dma(sy, dDBG, taps["d_xnT"][:], xnT[:])
            if phase_limit < 1:
                wait_all_dma(sy)
                wait_all_dma(gp)
                return nc

            with ExitStack() as stR:
                eR = stR.enter_context
                rre = eR(nc.sbuf_tensor("rre", [8, 512], F32))
                rt_sb = eR(nc.sbuf_tensor("rt_sb", [128, 8], F32))
                rs_sb = eR(nc.sbuf_tensor("rs_sb", [128, 8], F32))
                w_sb = eR(nc.sbuf_tensor("w_sb", [128, 8], F32))
                rstat = eR(nc.sbuf_tensor("rstat", [128, 4], F32))
                pR = eR(nc.psum_tensor("pR", [8, 512], F32))
                pRT = eR(nc.psum_tensor("pRT", [128, 8], F32))
                pWT = eR(nc.psum_tensor("pWT", [8, 128], F32))

                s_rr = sS.n
                t_rt_last = tS.n
                W(tn, vS, v_ln_done)
                W(tn, dG, d_pre2)
                W(tn, gS, g_wa)
                for m in range(NCH):
                    W(tn, sS, s_rr)
                    for ci in range(CT):
                        ins = tn.matmul(pR[:], rwr_sb[:, ci, :],
                                        xnT[:, ci, 512 * m:512 * (m + 1)],
                                        start=(ci == 0), stop=(ci == CT - 1))
                    t_rm = tS.inc(ins)
                    W(sl, tS, t_rm)
                    W(sl, tS, t_rt_last)
                    s_rr = ss.activation(out=rre[:], in_=pR[:], func=AF.Identity,
                                         bias=rbr_sb[:], scale=1.0)
                    for k in range(4):
                        stt = 4 * m + k
                        W(tn, sS, s_rr)
                        ins = tn.transpose(pRT[:], rre[:, 128 * k:128 * (k + 1)],
                                           identf[0:8, 0:8])
                        t_rt_last = tS.inc(ins)
                        W(ve, tS, t_rt_last)
                        vv.tensor_copy(out=rt_sb[:], in_=pRT[:])
                        vv.reduce_max(out=rstat[:, 0:1], in_=rt_sb[:], axis=AX.X)
                        v_rm = vv.tensor_scalar(out=rstat[:, 1:2], in0=rstat[:, 0:1],
                                                scalar1=-1.0, scalar2=None,
                                                op0=ALU.mult)
                        W(sl, vS, v_rm)
                        s_re = ss.activation(out=rs_sb[:], in_=rt_sb[:],
                                             func=AF.Exp, bias=rstat[:, 1:2],
                                             scale=1.0)
                        W(ve, sS, s_re)
                        vv.reduce_sum(out=rstat[:, 2:3], in_=rs_sb[:], axis=AX.X)
                        vv.reciprocal(out=rstat[:, 3:4], in_=rstat[:, 2:3])
                        vv.tensor_scalar(out=rs_sb[:], in0=rs_sb[:],
                                         scalar1=rstat[:, 3:4], scalar2=None,
                                         op0=ALU.mult)
                        vv.reduce_max(out=rstat[:, 0:1], in_=rs_sb[:], axis=AX.X)
                        vv.tensor_scalar(out=rt_sb[:], in0=rs_sb[:],
                                         scalar1=rstat[:, 0:1], scalar2=None,
                                         op0=ALU.is_equal)
                        vv.tensor_tensor(out=w_sb[:], in0=rs_sb[:], in1=rt_sb[:],
                                         op=ALU.mult)
                        vv.tensor_tensor(out=w_sb[:], in0=rs_sb[:], in1=w_sb[:],
                                         op=ALU.subtract)
                        vv.reduce_max(out=rstat[:, 1:2], in_=w_sb[:], axis=AX.X)
                        vv.tensor_scalar(out=w_sb[:], in0=w_sb[:],
                                         scalar1=rstat[:, 1:2], scalar2=None,
                                         op0=ALU.is_equal)
                        vv.tensor_tensor(out=rt_sb[:], in0=rt_sb[:], in1=w_sb[:],
                                         op=ALU.add)
                        v_w = vv.tensor_tensor(out=w_sb[:], in0=rs_sb[:],
                                               in1=rt_sb[:], op=ALU.mult)
                        W(tn, vS, v_w)
                        ins = tn.transpose(pWT[:], w_sb[:], identf[:])
                        t_wt = tS.inc(ins)
                        W(ve, tS, t_wt)
                        vv.tensor_copy(out=W_aug[0:8, 128 * stt:128 * (stt + 1)],
                                       in_=pWT[:])
                v_W_done = vS.n
            if dbg:
                W(sy, vS, v_W_done)
                W(sy, dDBG, dDBG.n)
                dma(sy, dDBG, taps["d_W"][:], W_aug[:])
            if phase_limit < 2:
                wait_all_dma(sy)
                wait_all_dma(gp)
                return nc

            with ExitStack() as stMM:
                eMM = stMM.enter_context
                QT = [eMM(nc.sbuf_tensor(f"QT{h}", [64, T], BF16)) for h in range(2)]
                KT = [eMM(nc.sbuf_tensor(f"KT{h}", [64, T], BF16)) for h in range(2)]
                VT = eMM(nc.sbuf_tensor("VT", [128, T], BF16))
                Vaug = eMM(nc.sbuf_tensor("Vaug", [128, TT, 66], BF16))
                P2 = eMM(nc.sbuf_tensor("P2", [128, 2, 1024], BF16))
                wbuf = eMM(nc.sbuf_tensor("wbuf", [128, 2, CT, 128], BF16))
                recro = eMM(nc.sbuf_tensor("recro", [128, 512], F32))
                stA_ = eMM(nc.sbuf_tensor("stA", [128, 16], F32))
                csml = eMM(nc.sbuf_tensor("csml", [64, 2, 512], BF16))
                cunb = eMM(nc.sbuf_tensor("cunb", [65, 2, 512], F32))
                acc = eMM(nc.sbuf_tensor("acc", [128, DT, FW], F32))
                H = eMM(nc.sbuf_tensor("H", [128, 2, FT, FW], BF16))
                rw1r = eMM(nc.sbuf_tensor("rw1r", [128, 2, 2, CT, 128], BF16))
                rw2r = eMM(nc.sbuf_tensor("rw2r", [128, 2, FT, 128], BF16))
                bcw = eMM(nc.sbuf_tensor("bcw", [128, 2, FW], BF16))
                pS = [eMM(nc.psum_tensor(f"pS{i}", [128, 1024], F32)) for i in range(2)]
                pC = eMM(nc.psum_tensor("pC", [65, 512], F32))
                pH = [eMM(nc.psum_tensor("pH0", [128, 512], F32))]
                pSM = eMM(nc.psum_tensor("pSM", [128, 512], F32))
                pVb = eMM(nc.psum_tensor("pVb", [128, 512], BF16))

                W(gp, gS, g_pre)
                W(gp, vS, vS.n)
                W(gp, sS, sS.n)
                W(gp, tS, tS.n)
                wait_all_dma(gp)
                g_rows = gg.memset(Vaug[:, :, 64:65], 1.0)

                precast = [w for i in range(E9)
                           for w in ((rw1_d[i], rw1b_d[i]), (rw2_d[i], rw2b_d[i]))]
                d_wcast = [0]

                fs = {
                    "v_pH": [0], "t_u_used": {}, "d_u": {},
                    "t_r2_used": {}, "d_r2": {}, "v_hc": {},
                    "s_gel": {}, "v_wm": {}, "t_sm": {},
                    "v_accdone": {}, "d_spill": {0: 0},
                    "v_evSM": 0, "s_evSM": 0, "s_bcw": {},
                    "nui": 2, "nsmi": 2,
                }
                NU = FT // 2

                def issue_u(ui):
                    g, r = divmod(ui, NU)
                    c, ei = divmod(g, E9)
                    if c >= FCH:
                        return
                    if ui - 2 in fs["t_u_used"]:
                        W(gp, tS, fs["t_u_used"][ui - 2])
                    W(gp, dPC, d_wcast[0])
                    fs["d_u"][ui] = dma(
                        gp, dW1[ui % 2], rw1r[:, ui % 2, :, :, :],
                        rw1b_d[ei][:, 2 * r:2 * r + 2, :, :])

                def issue_r2(smi):
                    g, d = divmod(smi, DT)
                    c, ei = divmod(g, E9)
                    if c >= FCH:
                        return
                    if smi - 2 in fs["t_r2_used"]:
                        W(gp, tS, fs["t_r2_used"][smi - 2])
                    W(gp, dPC, d_wcast[0])
                    fs["d_r2"][smi] = dma(
                        gp, dR2[smi % 2], rw2r[:, smi % 2, :, :],
                        rw2b_d[ei][:, d, :, :])

                def blk_bcw(c, ei):
                    g = c * E9 + ei
                    W(tn, vS, fs["v_wm"].get(g - 2, 0))
                    W(tn, vS, fs["v_evSM"])
                    W(tn, sS, fs["s_evSM"])
                    ins = tn.matmul(pSM[:], sel_sb[:, 128 * ei:128 * (ei + 1)],
                                    W_aug[:, FW * c:FW * (c + 1)],
                                    start=True, stop=True)
                    t_ = tS.inc(ins)
                    W(sl, tS, t_)
                    fs["s_bcw"][g] = ss.activation(
                        out=bcw[:, g % 2, :], in_=pSM[:], func=AF.Identity,
                        scale=1.0)
                    fs["s_evSM"] = fs["s_bcw"][g]

                def blk_fm(c, ei, fj):
                    g = c * E9 + ei
                    ui = g * NU + fj // 2
                    W(tn, dW1[ui % 2], fs["d_u"][ui])
                    W(tn, vS, fs["v_pH"][0])
                    for ci in range(CT):
                        ins = tn.matmul(pH[0][:],
                                        rw1r[:, ui % 2, fj % 2, ci, :],
                                        xnT[:, ci, FW * c:FW * (c + 1)],
                                        start=(ci == 0), stop=(ci == CT - 1))
                    t_fm = tS.inc(ins)
                    fs["t_u_used"][ui] = t_fm
                    if fj % 2 == 1 and fs["nui"] == ui + 2:
                        issue_u(fs["nui"])
                        fs["nui"] += 1
                    W(ve, tS, t_fm)
                    if g - 2 in fs["t_sm"]:
                        W(ve, tS, fs["t_sm"][g - 2])
                    fs["v_hc"][(g, fj)] = vv.tensor_copy(
                        out=H[:, g % 2, fj, :], in_=pH[0][:])
                    fs["v_pH"][0] = fs["v_hc"][(g, fj)]

                def flush_gelus(pend):
                    for g in pend:
                        fs.setdefault("flushed_at", {})[g] = cno[0]
                        c, ei = divmod(g, E9)
                        last_v = 0
                        for fj in range(FT):
                            W(sl, vS, fs["v_hc"][(g, fj)])
                            last_s = ss.activation(out=H[:, g % 2, fj, :],
                                                   in_=H[:, g % 2, fj, :],
                                                   func=AF.Gelu,
                                                   bias=rb1_sb[:, ei, fj:fj + 1],
                                                   scale=1.0)
                            if ei < 8:
                                W(ve, sS, last_s)
                                last_v = vv.tensor_tensor(
                                    out=H[:, g % 2, fj, :],
                                    in0=H[:, g % 2, fj, :],
                                    in1=bcw[:, g % 2, :], op=ALU.mult)
                        fs["s_gel"][g] = last_s
                        fs["v_wm"][g] = last_v

                def blk_sm(c, ei, d):
                    g = c * E9 + ei
                    smi = g * DT + d
                    W(tn, dR2[smi % 2], fs["d_r2"][smi])
                    W(tn, sS, fs["s_gel"][g])
                    W(tn, vS, fs["v_wm"][g])
                    W(tn, vS, fs["v_evSM"])
                    W(tn, sS, fs["s_evSM"])
                    for fj in range(FT):
                        ins = tn.matmul(pSM[:], rw2r[:, smi % 2, fj, :],
                                        H[:, g % 2, fj, :],
                                        start=(fj == 0), stop=(fj == FT - 1))
                    t_sm = tS.inc(ins)
                    fs["t_r2_used"][smi] = t_sm
                    if d == DT - 1:
                        fs["t_sm"][g] = t_sm
                    if fs["nsmi"] == smi + 2:
                        issue_r2(fs["nsmi"])
                        fs["nsmi"] += 1
                    W(ve, tS, t_sm)
                    if ei == 0:
                        W(ve, dSP, fs["d_spill"].get(c, 0))
                        fs["v_evSM"] = vv.tensor_copy(out=acc[:, d, :], in_=pSM[:])
                    else:
                        fs["v_evSM"] = vv.tensor_tensor(
                            out=acc[:, d, :], in0=acc[:, d, :], in1=pSM[:],
                            op=ALU.add)

                def blk_rb2s(c, d):
                    W(tn, vS, fs["v_evSM"])
                    W(tn, sS, fs["s_evSM"])
                    ins = tn.matmul(pSM[:], rb2_sb[:, 128 * d:128 * (d + 1)],
                                    W_aug[:, FW * c:FW * (c + 1)],
                                    start=True, stop=True)
                    t_ = tS.inc(ins)
                    W(ve, tS, t_)
                    fs["v_evSM"] = vv.tensor_tensor(
                        out=acc[:, d, :], in0=acc[:, d, :], in1=pSM[:], op=ALU.add)
                    if d == DT - 1:
                        fs["v_accdone"][c] = fs["v_evSM"]

                def blk_spill(c):
                    W(sy, vS, fs["v_accdone"][c])
                    for d in range(DT):
                        dma(sy, dSP, ffn_d[d][:, FW * c:FW * (c + 1)], acc[:, d, :])
                    fs["d_spill"][c + 1] = dSP.n

                fm_lane = []
                sm_lane = []
                for c in range(FCH):
                    for ei in range(E9):
                        if ei < 8:
                            fm_lane.append(("bcw", c, ei))
                        for fj in range(FT):
                            fm_lane.append(("fm", c, ei, fj))
                        for d in range(DT):
                            sm_lane.append(("sm", c, ei, d))
                    for d in range(DT):
                        sm_lane.append(("rb2s", c, d))
                    sm_lane.append(("spill", c))
                total_blocks = len(fm_lane) + len(sm_lane)
                cur = {"fm": 0, "sm": 0, "emitted": 0}
                pending_gelu = []

                def sm_ready(b):
                    fa = fs.setdefault("flushed_at", {})

                    def ready_g(g):
                        return g in fs["s_gel"] and cno[0] > fa.get(g, 1 << 30)
                    if b[0] == "sm":
                        return ready_g(b[1] * E9 + b[2])
                    if b[0] == "rb2s":
                        return ready_g(b[1] * E9 + E9 - 1)
                    if b[0] == "spill":
                        return b[1] in fs["v_accdone"]
                    return True

                def fm_ready(b):
                    g = b[1] * E9 + b[2]
                    if b[0] == "fm" and b[3] == 0:
                        if g >= 2 and (g - 2) not in fs["t_sm"]:
                            return False
                    if b[0] == "bcw":
                        if g >= 2 and (g - 2) not in fs["v_wm"]:
                            return False
                    return True

                def run_block(b):
                    kind = b[0]
                    if kind == "bcw":
                        blk_bcw(b[1], b[2])
                    elif kind == "fm":
                        blk_fm(b[1], b[2], b[3])
                        if b[3] == FT - 1:
                            pending_gelu.append(b[1] * E9 + b[2])
                    elif kind == "sm":
                        blk_sm(b[1], b[2], b[3])
                    elif kind == "rb2s":
                        blk_rb2s(b[1], b[2])
                    elif kind == "spill":
                        blk_spill(b[1])
                    cur["emitted"] += 1

                def pump(target):
                    progress = True
                    while cur["emitted"] < target and progress:
                        progress = False
                        while cur["sm"] < len(sm_lane) and cur["emitted"] < target:
                            b = sm_lane[cur["sm"]]
                            if not sm_ready(b):
                                break
                            run_block(b)
                            cur["sm"] += 1
                            progress = True
                        while cur["fm"] < len(fm_lane) and cur["emitted"] < target:
                            b = fm_lane[cur["fm"]]
                            if not fm_ready(b):
                                break
                            run_block(b)
                            cur["fm"] += 1
                            progress = True

                t_projdone = [0] * NP
                prev_exp = None
                prev_tPV = None
                v_ctx_prev = 0
                d_cs = [0, 0]
                pending_norm = []

                def do_norm(item):
                    h_idx, q_, v_rec_ = item
                    sl_ = q_ % 2
                    W(tn, vS, v_rec_)
                    W(tn, vS, fs["v_evSM"])
                    W(tn, sS, fs["s_evSM"])
                    ins_ = tn.matmul(pSM[0:64, :],
                                     ones_all[64 * sl_:64 * sl_ + 1, 0:64],
                                     recro[64 * sl_:64 * sl_ + 1, :],
                                     start=True, stop=True)
                    t_bc = tS.inc(ins_)
                    W(ve, tS, t_bc)
                    vv.tensor_copy(out=recro[0:64, :], in_=pSM[0:64, :])
                    fs["v_evSM"] = vS.n
                    W(ve, dR[sl_], d_cs[sl_])
                    v_cm = vv.tensor_tensor(
                        out=csml[:, sl_, :],
                        in0=cunb[0:64, sl_, :], in1=recro[0:64, :],
                        op=ALU.mult)
                    W(sy, vS, v_cm)
                    d_cs[sl_] = dma(sy, dR[sl_],
                                    ctx_d[h_idx][:, 512 * q_:512 * (q_ + 1)],
                                    csml[:, sl_, :])
                n_attn_chunks = NP * 2 * NCH
                chunk_no = 0
                cno = [0]
                total_kts = n_attn_chunks * 32
                kt_no = [0]
                kt_start = 16 * 32

                def _tgt():
                    k = kt_no[0]
                    if not d_wcast[0] or k < kt_start:
                        return 0
                    return total_blocks * (k - kt_start) // (total_kts - kt_start)

                def micro_pump():
                    kt_no[0] += 1
                    if kt_no[0] % 2 == 0:
                        pump(min(_tgt(), cur["emitted"] + 2))

                def fill(n=2):
                    pump(min(_tgt(), cur["emitted"] + n))

                for p in range(NP):
                    h0, h1 = 2 * p, 2 * p + 1
                    if p >= 1:
                        W(gp, tS, t_projdone[p - 1])
                    dma(gp, dG, wbuf[:, 0, :, :], wq_d[p])
                    dma(gp, dG, wbuf[:, 1, :, :], wk_d[p])
                    d_wqk = dG.n
                    W(ve, dG, d_wqk)
                    W(ve, vS, v_pre)
                    vw0 = vv.tensor_scalar(out=wbuf[:, 0, :, :], in0=wbuf[:, 0, :, :],
                                           scalar1=scol[:, 0:1], scalar2=None,
                                           op0=ALU.mult)
                    vw1 = vv.tensor_scalar(out=wbuf[:, 1, :, :], in0=wbuf[:, 1, :, :],
                                           scalar1=scol[:, 1:2], scalar2=None,
                                           op0=ALU.mult)

                    slots = [(pS[0], 0), (pS[0], 512), (pS[1], 0), (pS[1], 512)]
                    slot_tick = [sS.n] * 4
                    kch = 0

                    def proj(mat, lo_dst, hi_dst, b_sb, v_w, hi_wait=0):
                        nonlocal kch
                        first = tS.n
                        W(tn, vS, v_w)
                        W(tn, vS, v_ln_done)
                        W(tn, gS, g_rows)
                        for m in range(NCH):
                            ps, off = slots[kch % 4]
                            W(tn, sS, slot_tick[kch % 4])
                            for ci in range(CT):
                                ins = tn.matmul(
                                    ps[:, off:off + 512], wbuf[:, mat, ci, :],
                                    xnT[:, ci, 512 * m:512 * (m + 1)],
                                    start=(ci == 0), stop=(ci == CT - 1))
                            t_mm = tS.inc(ins)
                            if m == 0:
                                first = t_mm
                            W(sl, tS, t_mm)
                            if hi_wait:
                                W(sl, dRB, hi_wait)
                            ss.activation(out=lo_dst[0:64, 512 * m:512 * (m + 1)],
                                          in_=ps[0:64, off:off + 512],
                                          func=AF.Identity,
                                          bias=b_sb[0:64, h0:h0 + 1], scale=1.0)
                            slot_tick[kch % 4] = ss.activation(
                                out=hi_dst[64:128, 512 * m:512 * (m + 1)],
                                in_=ps[64:128, off:off + 512], func=AF.Identity,
                                bias=b_sb[64:128, h1:h1 + 1], scale=1.0)
                            kch += 1
                        return slot_tick[(kch - 1) % 4], first

                    s_q, t_q0 = proj(0, QT[0], VT, bq_sb, vw0)
                    W(gp, sS, s_q)
                    d_q1 = dma(gp, dRB, QT[1][:], VT[64:128, :])
                    W(gp, tS, tS.n)
                    dma(gp, dG, wbuf[:, 0, :, :], wv_d[p])
                    d_wv = dG.n
                    W(ve, dG, d_wv)
                    vwv = vv.tensor_scalar(out=wbuf[:, 0, :, :], in0=wbuf[:, 0, :, :],
                                           scalar1=scol[:, 2:3], scalar2=None,
                                           op0=ALU.mult)
                    s_k, _ = proj(1, KT[0], VT, bk_sb, vw1, hi_wait=d_q1)
                    W(gp, sS, s_k)
                    d_k1 = dma(gp, dRB, KT[1][:], VT[64:128, :])
                    s_v, _ = proj(0, VT, VT, bv_sb, vwv, hi_wait=d_k1)
                    t_projdone[p] = tS.n
                    if p == 0:
                        for src, dst in precast:
                            dma(gp, dPC, dst, src)
                        precast = []
                        d_wcast[0] = dPC.n
                        issue_u(0)
                        issue_u(1)
                        issue_r2(0)
                        issue_r2(1)

                    for hh in range(2):
                        qt, kt = QT[hh], KT[hh]
                        idsl = ident[0:64, 0:64] if hh == 0 else \
                            ident[64:128, 64:128]
                        v_vcp = vS.n
                        W(tn, sS, s_v)
                        for gvt in range(4):
                            fill()
                            W(tn, vS, v_vcp)
                            for j in range(8):
                                kti = 8 * gvt + j
                                ins = tn.transpose(
                                    pVb[:, 64 * j:64 * (j + 1)],
                                    VT[64 * hh:64 * (hh + 1),
                                       128 * kti:128 * (kti + 1)],
                                    idsl)
                            t_v = tS.inc(ins)
                            W(ve, tS, t_v)
                            v_vcp = vv.tensor_copy(
                                out=Vaug[:, 8 * gvt:8 * (gvt + 1), 0:64],
                                in_=pVb[:].rearrange("p (a b) -> p a b", a=8))
                        v_vaug = v_vcp

                        if hh == 1:
                            W(ve, dRB, d_k1)
                        for which, tt in ((0, qt), (1, kt)):
                            W(ve, sS, s_q if which == 0 else s_k)
                            v_am = vv.reduce_max(out=stA_[0:64, 0:1], in_=tt[:],
                                                 axis=AX.X,
                                                 apply_absolute_value=True)
                            fill()
                            W(tn, vS, v_am)
                            W(tn, vS, v_ctx_prev)
                            ins = tn.transpose(pC[0:1, 0:64], stA_[0:64, 0:1],
                                               identf[0:64, 0:64])
                            t_am = tS.inc(ins)
                            W(ve, tS, t_am)
                            vv.reduce_max(out=stA_[0:1, 8 + which:9 + which],
                                          in_=pC[0:1, 0:64], axis=AX.X)
                            fill()
                        vv.tensor_tensor(out=stA_[0:1, 10:11], in0=stA_[0:1, 8:9],
                                         in1=stA_[0:1, 9:10], op=ALU.mult)
                        v_negc = vv.tensor_scalar(out=stA_[0:1, 13:14],
                                                  in0=stA_[0:1, 10:11],
                                                  scalar1=-64.0, scalar2=None,
                                                  op0=ALU.mult)
                        W(tn, vS, v_negc)
                        W(tn, sS, sS.n)
                        ins = tn.matmul(pS[0][:, 0:1], ones_r128[:],
                                        stA_[0:1, 13:14], start=True, stop=True)
                        t_nc = tS.inc(ins)
                        W(ve, tS, t_nc)
                        W(ve, sS, sS.n)
                        v_ncol = vv.tensor_copy(out=stA_[:, 15:16],
                                                in_=pS[0][:, 0:1])

                        W(tn, vS, v_ncol)
                        W(tn, vS, v_vaug)
                        for q in range(NCH):
                            s_exp = [0] * 16
                            t_PV = [0] * 16
                            for j in range(16):
                                if j >= 2:
                                    W(tn, sS, s_exp[j - 2])
                                elif prev_exp is not None:
                                    W(tn, sS, prev_exp[14 + j])
                                for half in range(2):
                                    kt_i = 2 * j + half
                                    ins = tn.matmul(
                                        pS[j % 2][:, 512 * half:512 * (half + 1)],
                                        kt[:, 128 * kt_i:128 * (kt_i + 1)],
                                        qt[:, 512 * q:512 * (q + 1)],
                                        start=True, stop=True)
                                t_S = tS.inc(ins)
                                if j >= 1:
                                    jj = j - 1
                                    W(tn, sS, s_exp[jj])
                                    if jj == 0:
                                        W(tn, vS, v_ctx_prev)
                                    for half in range(2):
                                        kt_i = 2 * jj + half
                                        ins = tn.matmul(
                                            pC[:], Vaug[:, kt_i, 0:65],
                                            P2[:, jj % 2,
                                               512 * half:512 * (half + 1)],
                                            start=(kt_i == 0), stop=False)
                                    t_PV[jj] = tS.inc(ins)
                                W(sl, tS, t_S)
                                if j >= 2:
                                    W(sl, tS, t_PV[j - 2])
                                elif prev_tPV is not None:
                                    W(sl, tS, prev_tPV[14 + j])
                                s_exp[j] = ss.activation(
                                    out=P2[:, j % 2, :], in_=pS[j % 2][:],
                                    func=AF.Exp, bias=stA_[:, 15:16], scale=1.0)
                                if j == 4 and pending_norm:
                                    do_norm(pending_norm.pop(0))
                                micro_pump()
                                micro_pump()
                            W(tn, sS, s_exp[15])
                            for half in range(2):
                                ins = tn.matmul(pC[:], Vaug[:, 30 + half, 0:65],
                                                P2[:, 1,
                                                   512 * half:512 * (half + 1)],
                                                start=False, stop=(half == 1))
                            t_pvl = tS.inc(ins)
                            W(ve, tS, t_pvl)
                            v_ctx_prev = vv.tensor_copy(out=cunb[:, q % 2, :],
                                                        in_=pC[:])
                            v_rec = vv.reciprocal(out=recro[64 * (q % 2):
                                                            64 * (q % 2) + 1, :],
                                                  in_=cunb[64:65, q % 2, :])
                            pending_norm.append((2 * p + hh, q, v_rec))
                            prev_exp = s_exp
                            prev_tPV = t_PV
                            chunk_no += 1
                            cno[0] = chunk_no
                            if d_wcast[0] and pending_gelu:
                                flush_gelus(pending_gelu)
                                pending_gelu.clear()

                while pending_norm:
                    do_norm(pending_norm.pop(0))
                cno[0] = 1 << 31
                guard = 0
                while cur["emitted"] < total_blocks:
                    if pending_gelu:
                        flush_gelus(pending_gelu)
                        pending_gelu.clear()
                    before = cur["emitted"]
                    pump(total_blocks)
                    if cur["emitted"] == before and not pending_gelu:
                        raise RuntimeError("ffn scheduler deadlock")
                    guard += 1
                    if guard > 100000:
                        raise RuntimeError("ffn scheduler livelock")
                d_cs_done = (dR[0].n, dR[1].n)
                d_sp_done = dSP.n
                t_merged = tS.n
                v_merged = vS.n
                s_merged = sS.n

        if phase_limit < 3:
            wait_all_dma(sy)
            wait_all_dma(gp)
            return nc

        with ExitStack() as stF:
            eF = stF.enter_context
            ctx_sb = eF(nc.sbuf_tensor("ctx_sb", [128, CT, T], BF16))
            wo_sb = eF(nc.sbuf_tensor("wo_sb", [128, CT, DT, 128], BF16))
            ffn_sb = eF(nc.sbuf_tensor("ffn_sb", [128, DT, 1024], F32))
            xt2c = eF(nc.sbuf_tensor("xt2c", [128, 2, D], F32))
            ostg = eF(nc.sbuf_tensor("ostg", [128, 2, D], F32))
            pO = [eF(nc.psum_tensor(f"pO{i}", [128, 512], F32)) for i in range(2)]
            pD = [eF(nc.psum_tensor(f"pD{i}", [128, 512], F32)) for i in range(2)]

            W(gp, tS, t_merged)
            W(gp, vS, v_merged)
            W(gp, sS, s_merged)
            wait_all_dma(gp)
            dma(gp, dG, wo_sb[:], wo_d[:])
            d_wo = dG.n
            W(ve, dG, d_wo)
            v_wo = vv.tensor_scalar(
                out=wo_sb[:].rearrange("p a b c -> p (a b c)"),
                in0=wo_sb[:].rearrange("p a b c -> p (a b c)"),
                scalar1=scol[:, 3:4], scalar2=None, op0=ALU.mult)
            W(sy, tS, t_merged)
            W(sy, vS, v_merged)
            W(sy, sS, s_merged)
            W(sy, dR[0], d_cs_done[0])
            W(sy, dR[1], d_cs_done[1])
            for h in range(16):
                dma(sy, dCX, ctx_sb[64 * (h % 2):64 * (h % 2) + 64, h // 2, :],
                    ctx_d[h])
            d_ctx_l = dCX.n

            v_pDt = [0, 0]
            v_addfull = [vS.n] * 2
            d_og = [0, 0]
            t_pD_done = 0
            for c in range(4):
                W(sy, dSP, d_sp_done)
                W(sy, tS, t_pD_done)
                for d in range(DT):
                    dma(sy, dFB, ffn_sb[:, d, :],
                        ffn_d[d][:, 1024 * c:1024 * (c + 1)])
                d_fl = dFB.n
                v_ev = [vS.n] * 2
                W(tn, vS, v_wo)
                W(tn, dCX, d_ctx_l)
                for d in range(DT):
                    for m in range(2):
                        W(tn, vS, v_ev[m])
                        for ci in range(CT):
                            ins = tn.matmul(
                                pO[m][:], wo_sb[:, ci, d, :],
                                ctx_sb[:, ci, 1024 * c + 512 * m:
                                       1024 * c + 512 * (m + 1)],
                                start=(ci == 0), stop=(ci == CT - 1))
                        t_mm = tS.inc(ins)
                        W(ve, tS, t_mm)
                        W(ve, dFB, d_fl)
                        v_ev[m] = vv.scalar_tensor_tensor(
                            out=ffn_sb[:, d, 512 * m:512 * (m + 1)],
                            in0=pO[m][:], scalar=bo_sb[:, d:d + 1],
                            in1=ffn_sb[:, d, 512 * m:512 * (m + 1)],
                            op0=ALU.add, op1=ALU.add)
                v_comb = max(v_ev)
                for stt in range(8):
                    buf = stt % 2
                    row = 128 * (8 * c + stt)
                    W(sy, vS, v_addfull[buf])
                    d_xt = dma(sy, dX[buf], xt2c[:, buf, :], x_d[row:row + 128, :])
                    for r in range(2):
                        W(tn, vS, v_pDt[r])
                        W(tn, vS, v_comb)
                        for dd in range(4):
                            d = 4 * r + dd
                            ins = tn.transpose(
                                pD[r][:, 128 * dd:128 * (dd + 1)],
                                ffn_sb[:, d, 128 * stt:128 * (stt + 1)],
                                identf[:])
                        t_tp = tS.inc(ins)
                        t_pD_done = t_tp
                        W(ve, tS, t_tp)
                        W(ve, dX[buf], d_xt)
                        W(ve, dOut[buf], d_og[buf])
                        v_pDt[r] = vv.tensor_tensor(
                            out=ostg[:, buf, 512 * r:512 * (r + 1)],
                            in0=xt2c[:, buf, 512 * r:512 * (r + 1)],
                            in1=pD[r][:], op=ALU.add)
                    v_addfull[buf] = vS.n
                    W(sy, vS, v_addfull[buf])
                    d_og[buf] = dma(sy, dOut[buf], out_d[row:row + 128, :],
                                    ostg[:, buf, :])

        wait_all_dma(sy)
        wait_all_dma(gp)
    return nc


def _host_layouts(inp):
    f = np.float32

    def a(k):
        return np.asarray(inp[k], dtype=f)

    wq = a("L0_attn_w_q"); wk = a("L0_attn_w_k")
    wv = a("L0_attn_w_v"); wo = a("L0_attn_w_o")

    def wtile(w):
        return np.ascontiguousarray(
            w.T.reshape(8, 128, 8, 128).transpose(2, 1, 0, 3))

    def btile(b):
        t = b.reshape(16, 64).T
        return np.ascontiguousarray(np.concatenate([t, t], axis=0))

    rw1 = a("L0_ffn_rw1"); sw1 = a("L0_ffn_sw1")
    rw2 = a("L0_ffn_rw2"); sw2 = a("L0_ffn_sw2")
    rb1 = a("L0_ffn_rb1"); sb1 = a("L0_ffn_sb1")
    rb2 = a("L0_ffn_rb2"); sb2 = a("L0_ffn_sb2")
    rw1s = np.concatenate([rw1, sw1[None]], axis=0)
    rw2s = np.concatenate([rw2, sw2[None]], axis=0)
    rb1s = np.concatenate([rb1, sb1[None]], axis=0)
    rb2s = np.concatenate([rb2, sb2[None]], axis=0)

    shared = {
        "wq_t": wtile(wq), "wk_t": wtile(wk), "wv_t": wtile(wv),
        "wo_t": np.ascontiguousarray(
            wo.T.reshape(8, 128, 8, 128).transpose(1, 0, 2, 3)),
        "bq_t": btile(a("L0_attn_b_q")), "bk_t": btile(a("L0_attn_b_k")),
        "bv_t": btile(a("L0_attn_b_v")),
        "bo_t": np.ascontiguousarray(a("L0_attn_b_o").reshape(8, 128).T),
        "s4": np.stack([a("L0_attn_s_q").reshape(()), a("L0_attn_s_k").reshape(()),
                        a("L0_attn_s_v").reshape(()), a("L0_attn_s_o").reshape(())]
                       ).reshape(1, 4).astype(f),
        "rw1_t": np.ascontiguousarray(
            rw1s.reshape(9, 8, 128, 16, 128).transpose(0, 2, 3, 1, 4)),
        "rw2_t": np.ascontiguousarray(
            rw2s.reshape(9, 16, 128, 8, 128).transpose(0, 2, 3, 1, 4)),
        "rb1_t": np.ascontiguousarray(rb1s.reshape(9, 16, 128).transpose(2, 0, 1)),
        "rb2s": np.ascontiguousarray(rb2s),
        "rwr_t": np.ascontiguousarray(
            a("L0_ffn_router_w").reshape(8, 128, 8).transpose(1, 0, 2)),
        "rbr_t": np.ascontiguousarray(a("L0_ffn_router_b").reshape(8, 1)),
        "sel9": np.ascontiguousarray(
            np.repeat(np.eye(9, 8, dtype=f), 128, axis=1)),
    }
    x = np.asarray(inp["L0_attn_x"], dtype=f)
    return shared, x


def kernel(**inp):
    shared, x = _host_layouts(inp)
    nc = build_program(dbg=_DEBUG_TAPS)
    in_maps = []
    for b in range(8):
        m = dict(shared)
        m["x"] = np.ascontiguousarray(x[b])
        in_maps.append(m)
    res = run_bass_kernel_spmd(nc, in_maps, core_ids=list(range(8)))
    out = np.stack([np.asarray(r["out"], dtype=np.float32) for r in res.results],
                   axis=0)
    if _DEBUG_TAPS:
        kernel._last_results = res
    return out


# version 12 (final)
# speedup vs baseline: 1.0239x; 1.0032x over previous
import numpy as np
from contextlib import ExitStack

import concourse.bass as bass
import concourse.mybir as mybir
from concourse.bass_utils import run_bass_kernel_spmd

F32 = mybir.dt.float32
BF16 = mybir.dt.bfloat16
AF = mybir.ActivationFunctionType
ALU = mybir.AluOpType
AX = mybir.AxisListType

T = 4096
D = 1024
NP = 8
CT = 8
FT = 16
DT = 8
E9 = 9
NCH = 8
TT = 32
FCH = 8
FW = 512

_DEBUG_TAPS = False


class _Sem:
    def __init__(self, st, nc, name, step=1):
        self.h = st.enter_context(nc.semaphore(name))
        self.n = 0
        self.step = step

    def inc(self, instr):
        instr.then_inc(self.h, self.step)
        self.n += self.step
        return self.n


class _Waiter:
    def __init__(self):
        self.seen = {}

    def __call__(self, eng, sem, val=None):
        v = sem.n if val is None else val
        if v <= 0:
            return
        k = (id(eng), id(sem))
        if self.seen.get(k, 0) < v:
            eng.wait_ge(sem.h, v)
            self.seen[k] = v


class _Auto:

    def __init__(self, eng, sem, waiter, skip=("wait_ge", "dma_start")):
        self._e = eng
        self._s = sem
        self._W = waiter
        self._skip = set(skip)

    def __getattr__(self, name):
        f = getattr(self._e, name)
        if name in self._skip or name.startswith("_"):
            return f

        def g(*a, **k):
            self._W(self._e, self._s, self._s.n)
            ins = f(*a, **k)
            self._s.n += 1
            ins.then_inc(self._s.h, 1)
            return self._s.n

        return g


def build_program(dbg=False, phase_limit=99):
    nc = bass.Bass("TRN2", target_bir_lowering=False, debug=False)
    tn, sl, ve, gp, sy = nc.tensor, nc.scalar, nc.vector, nc.gpsimd, nc.sync

    x_d = nc.dram_tensor("x", [T, D], F32, kind="ExternalInput")
    wq_d = nc.dram_tensor("wq_t", [NP, 128, CT, 128], F32, kind="ExternalInput")
    wk_d = nc.dram_tensor("wk_t", [NP, 128, CT, 128], F32, kind="ExternalInput")
    wv_d = nc.dram_tensor("wv_t", [NP, 128, CT, 128], F32, kind="ExternalInput")
    wo_d = nc.dram_tensor("wo_t", [128, CT, DT, 128], F32, kind="ExternalInput")
    bq_d = nc.dram_tensor("bq_t", [128, 16], F32, kind="ExternalInput")
    bk_d = nc.dram_tensor("bk_t", [128, 16], F32, kind="ExternalInput")
    bv_d = nc.dram_tensor("bv_t", [128, 16], F32, kind="ExternalInput")
    bo_d = nc.dram_tensor("bo_t", [128, 8], F32, kind="ExternalInput")
    s4_d = nc.dram_tensor("s4", [1, 4], F32, kind="ExternalInput")
    rw1_d = nc.dram_tensor("rw1_t", [E9, 128, FT, CT, 128], F32, kind="ExternalInput")
    rw2_d = nc.dram_tensor("rw2_t", [E9, 128, DT, FT, 128], F32, kind="ExternalInput")
    rb1_d = nc.dram_tensor("rb1_t", [128, E9, FT], F32, kind="ExternalInput")
    rb2_d = nc.dram_tensor("rb2s", [E9, D], F32, kind="ExternalInput")
    rwr_d = nc.dram_tensor("rwr_t", [128, CT, 8], F32, kind="ExternalInput")
    rbr_d = nc.dram_tensor("rbr_t", [8, 1], F32, kind="ExternalInput")
    sel_d = nc.dram_tensor("sel9", [E9, 8 * 128], F32, kind="ExternalInput")
    out_d = nc.dram_tensor("out", [T, D], F32, kind="ExternalOutput")
    scrk = "ExternalOutput" if dbg else "Internal"
    ffn_d = nc.dram_tensor("ffn_scr", [DT, 128, T], F32, kind=scrk)
    ctx_d = nc.dram_tensor("ctx_scr", [16, 64, T], BF16, kind=scrk)
    rw1b_d = nc.dram_tensor("rw1b", [E9, 128, FT, CT, 128], BF16, kind="Internal")
    rw2b_d = nc.dram_tensor("rw2b", [E9, 128, DT, FT, 128], BF16, kind="Internal")
    wqb_d = nc.dram_tensor("wqb", [NP, 128, CT, 128], BF16, kind="Internal")
    wkb_d = nc.dram_tensor("wkb", [NP, 128, CT, 128], BF16, kind="Internal")
    wvb_d = nc.dram_tensor("wvb", [NP, 128, CT, 128], BF16, kind="Internal")
    taps = {}
    if dbg:
        taps["d_xnT"] = nc.dram_tensor("d_xnT", [128, CT, T], BF16, kind="ExternalOutput")
        taps["d_W"] = nc.dram_tensor("d_W", [E9, T], BF16, kind="ExternalOutput")

    W = _Waiter()
    with ExitStack() as st:
        e = st.enter_context
        tS = _Sem(st, nc, "tS")
        sS = _Sem(st, nc, "sS")
        vS = _Sem(st, nc, "vS")
        gS = _Sem(st, nc, "gS")
        dX = [_Sem(st, nc, f"dX{i}", 16) for i in range(2)]
        dR = [_Sem(st, nc, f"dR{i}", 16) for i in range(2)]
        dSP = _Sem(st, nc, "dSP", 16)
        dFB = _Sem(st, nc, "dFB", 16)
        dCX = _Sem(st, nc, "dCX", 16)
        dOut = [_Sem(st, nc, f"dOut{i}", 16) for i in range(2)]
        dR2 = [_Sem(st, nc, f"dR2{i}", 16) for i in range(2)]
        dWA = _Sem(st, nc, "dWA", 16)
        dDBG = _Sem(st, nc, "dDBG", 16)
        dG = _Sem(st, nc, "dG", 16)
        dRB = _Sem(st, nc, "dRB", 16)
        dW1 = [_Sem(st, nc, f"dW1{i}", 16) for i in range(2)]
        dPC = _Sem(st, nc, "dPC", 16)
        sync_sems = dX + dR + [dSP, dFB, dCX] + dOut + [dWA, dDBG]
        gp_sems = [dG, dRB, dPC] + dW1 + dR2

        def wait_all_dma(eng):
            for s_ in sync_sems + gp_sems:
                W(eng, s_, s_.n)

        def dma(q, qsem, out, in_):
            q.dma_start(out=out, in_=in_).then_inc(qsem.h, 16)
            qsem.n += 16
            return qsem.n

        vv = _Auto(ve, vS, W)
        ss = _Auto(sl, sS, W)
        gg = _Auto(gp, gS, W)

        ident = e(nc.sbuf_tensor("ident", [128, 128], BF16))
        identf = e(nc.sbuf_tensor("identf", [128, 128], F32))
        eps = e(nc.sbuf_tensor("eps", [128, 1], F32))
        ones_r128 = e(nc.sbuf_tensor("ones_r128", [1, 128], F32))
        ones_all = e(nc.sbuf_tensor("ones_all", [128, 64], F32))
        scol = e(nc.sbuf_tensor("scol", [128, 4], F32))
        s_sb = e(nc.sbuf_tensor("s_sb", [1, 4], F32))
        bq_sb = e(nc.sbuf_tensor("bq_sb", [128, 16], F32))
        bk_sb = e(nc.sbuf_tensor("bk_sb", [128, 16], F32))
        bv_sb = e(nc.sbuf_tensor("bv_sb", [128, 16], F32))
        bo_sb = e(nc.sbuf_tensor("bo_sb", [128, 8], F32))
        rwr_sb = e(nc.sbuf_tensor("rwr_sb", [128, CT, 8], BF16))
        rbr_sb = e(nc.sbuf_tensor("rbr_sb", [8, 1], F32))

        for dst, src in [(s_sb, s4_d), (bq_sb, bq_d), (bk_sb, bk_d), (bv_sb, bv_d),
                         (bo_sb, bo_d), (rbr_sb, rbr_d), (rwr_sb, rwr_d)]:
            dma(gp, dG, dst[:], src[:])
        d_pre = dG.n
        gg.memset(ident[:], 0.0)
        gg.affine_select(out=ident[:], in_=ident[:], compare_op=ALU.not_equal,
                         fill=1.0, base=0, pattern=[[-1, 128]], channel_multiplier=1)
        gg.memset(identf[:], 0.0)
        gg.affine_select(out=identf[:], in_=identf[:], compare_op=ALU.not_equal,
                         fill=1.0, base=0, pattern=[[-1, 128]], channel_multiplier=1)
        gg.memset(eps[:], 1e-5)
        gg.memset(ones_r128[:], 1.0)
        g_pre = gg.memset(ones_all[:], 1.0)

        with ExitStack() as stM:
            eM = stM.enter_context
            xnT = eM(nc.sbuf_tensor("xnT", [128, CT, T], BF16))
            W_aug = eM(nc.sbuf_tensor("W_aug", [E9, T], BF16))
            rb1_sb = eM(nc.sbuf_tensor("rb1_sb", [128, E9, FT], F32))
            rb2_sb = eM(nc.sbuf_tensor("rb2_sb", [E9, D], BF16))
            sel_sb = eM(nc.sbuf_tensor("sel_sb", [E9, 8 * 128], BF16))
            W(gp, dG, d_pre)
            dma(gp, dG, rb1_sb[:], rb1_d[:])
            dma(gp, dG, rb2_sb[:], rb2_d[:])
            dma(gp, dG, sel_sb[:], sel_d[:])
            d_pre2 = dG.n
            W(gp, gS, g_pre)
            g_wa = gg.memset(W_aug[:], 1.0)

            with ExitStack() as st0:
                e0 = st0.enter_context
                xt2 = e0(nc.sbuf_tensor("p0_x", [128, 2, D], F32))
                xnb = e0(nc.sbuf_tensor("p0_xn", [128, 2, D], BF16))
                st0_ = e0(nc.sbuf_tensor("p0_stat", [128, 8], F32))
                bns = e0(nc.sbuf_tensor("p0_bns", [128, 12], F32))
                pbf = e0(nc.psum_tensor("p0_pbf", [128, 2, 1024], BF16))
                ppre = e0(nc.psum_tensor("p0_pre", [128, 8], F32))

                W(tn, dG, d_pre)
                W(tn, gS, g_pre)
                for i in range(4):
                    ins = tn.matmul(ppre[:, i:i + 1], ones_r128[:], s_sb[0:1, i:i + 1],
                                    start=True, stop=True)
                t_pre = tS.inc(ins)
                W(ve, tS, t_pre)
                W(ve, dG, d_pre)
                vv.tensor_copy(out=scol[:], in_=ppre[:, 0:4])
                vv.tensor_scalar(out=scol[:, 0:1], in0=scol[:, 0:1], scalar1=0.125,
                                 scalar2=None, op0=ALU.mult)
                v_pre = vv.tensor_scalar(out=bq_sb[:], in0=bq_sb[:], scalar1=0.125,
                                         scalar2=None, op0=ALU.mult)

                v_xn = [0] * TT
                v_cp = [0] * TT
                d_x = [0] * TT
                d_x[0] = dma(sy, dX[0], xt2[:, 0, :], x_d[0:128, :])
                d_x[1] = dma(sy, dX[1], xt2[:, 1, :], x_d[128:256, :])
                v_stat = [0] * TT
                t_trs = [0] * TT
                for i in range(TT):
                    xt = xt2[:, i % 2, :]
                    if i >= 2:
                        W(ve, tS, t_trs[i - 2])
                        v_cp[i - 2] = vv.tensor_copy(
                            out=xnT[:, :, 128 * (i - 2):128 * (i - 1)],
                            in_=pbf[:, (i - 2) % 2, :]
                            .rearrange("p (c t) -> p c t", c=CT))
                    W(ve, dX[i % 2], d_x[i])
                    W(ve, vS, v_pre)
                    vv.bn_stats(out=bns[:, 0:6], in_=xt[:, 0:512])
                    vv.bn_stats(out=bns[:, 6:12], in_=xt[:, 512:1024])
                    vv.bn_aggr(out=st0_[:, 0:2], in_=bns[:])
                    v_st = vv.tensor_scalar(out=st0_[:, 4:5], in0=st0_[:, 0:1],
                                            scalar1=-1.0, scalar2=None,
                                            op0=ALU.mult)
                    v_stat[i] = v_st
                    W(sl, vS, v_st)
                    s_r = 0
                    ss.activation(out=st0_[:, 5:6], in_=st0_[:, 1:2], func=AF.Ln,
                                  bias=eps[:], scale=1.0)
                    s_r = ss.activation(out=st0_[:, 6:7], in_=st0_[:, 5:6],
                                        func=AF.Exp, scale=-0.5)
                    W(ve, sS, s_r)
                    v_xn[i] = vv.tensor_scalar(out=xnb[:, i % 2, :], in0=xt,
                                               scalar1=st0_[:, 4:5],
                                               scalar2=st0_[:, 6:7],
                                               op0=ALU.add, op1=ALU.mult)
                    if i + 2 < TT:
                        W(sy, vS, v_xn[i])
                        d_x[i + 2] = dma(sy, dX[i % 2], xt2[:, i % 2, :],
                                         x_d[128 * (i + 2):128 * (i + 3), :])
                    W(tn, vS, v_xn[i])
                    if i >= 2:
                        W(tn, vS, v_cp[i - 2])
                    for j in range(CT):
                        ins = tn.transpose(pbf[:, i % 2, 128 * j:128 * (j + 1)],
                                           xnb[:, i % 2, 128 * j:128 * (j + 1)],
                                           ident[:])
                    t_trs[i] = tS.inc(ins)
                for i in (TT - 2, TT - 1):
                    W(ve, tS, t_trs[i])
                    v_cp[i] = vv.tensor_copy(
                        out=xnT[:, :, 128 * i:128 * (i + 1)],
                        in_=pbf[:, i % 2, :].rearrange("p (c t) -> p c t", c=CT))
                v_ln_done = v_cp[TT - 1]

            if dbg:
                W(sy, vS, v_ln_done)
                W(sy, dDBG, dDBG.n)
                dma(sy, dDBG, taps["d_xnT"][:], xnT[:])
            if phase_limit < 1:
                wait_all_dma(sy)
                wait_all_dma(gp)
                return nc

            with ExitStack() as stR:
                eR = stR.enter_context
                rre = eR(nc.sbuf_tensor("rre", [8, 512], F32))
                rt_sb = eR(nc.sbuf_tensor("rt_sb", [128, 8], F32))
                rs_sb = eR(nc.sbuf_tensor("rs_sb", [128, 8], F32))
                w_sb = eR(nc.sbuf_tensor("w_sb", [128, 8], F32))
                rstat = eR(nc.sbuf_tensor("rstat", [128, 4], F32))
                pR = eR(nc.psum_tensor("pR", [8, 512], F32))
                pRT = eR(nc.psum_tensor("pRT", [128, 8], F32))
                pWT = eR(nc.psum_tensor("pWT", [8, 128], F32))

                s_rr = sS.n
                t_rt_last = tS.n
                W(tn, vS, v_ln_done)
                W(tn, dG, d_pre2)
                W(tn, gS, g_wa)
                for m in range(NCH):
                    W(tn, sS, s_rr)
                    for ci in range(CT):
                        ins = tn.matmul(pR[:], rwr_sb[:, ci, :],
                                        xnT[:, ci, 512 * m:512 * (m + 1)],
                                        start=(ci == 0), stop=(ci == CT - 1))
                    t_rm = tS.inc(ins)
                    W(sl, tS, t_rm)
                    W(sl, tS, t_rt_last)
                    s_rr = ss.activation(out=rre[:], in_=pR[:], func=AF.Identity,
                                         bias=rbr_sb[:], scale=1.0)
                    for k in range(4):
                        stt = 4 * m + k
                        W(tn, sS, s_rr)
                        ins = tn.transpose(pRT[:], rre[:, 128 * k:128 * (k + 1)],
                                           identf[0:8, 0:8])
                        t_rt_last = tS.inc(ins)
                        W(ve, tS, t_rt_last)
                        vv.tensor_copy(out=rt_sb[:], in_=pRT[:])
                        vv.reduce_max(out=rstat[:, 0:1], in_=rt_sb[:], axis=AX.X)
                        v_rm = vv.tensor_scalar(out=rstat[:, 1:2], in0=rstat[:, 0:1],
                                                scalar1=-1.0, scalar2=None,
                                                op0=ALU.mult)
                        W(sl, vS, v_rm)
                        s_re = ss.activation(out=rs_sb[:], in_=rt_sb[:],
                                             func=AF.Exp, bias=rstat[:, 1:2],
                                             scale=1.0)
                        W(ve, sS, s_re)
                        vv.reduce_sum(out=rstat[:, 2:3], in_=rs_sb[:], axis=AX.X)
                        vv.reciprocal(out=rstat[:, 3:4], in_=rstat[:, 2:3])
                        vv.tensor_scalar(out=rs_sb[:], in0=rs_sb[:],
                                         scalar1=rstat[:, 3:4], scalar2=None,
                                         op0=ALU.mult)
                        vv.reduce_max(out=rstat[:, 0:1], in_=rs_sb[:], axis=AX.X)
                        vv.tensor_scalar(out=rt_sb[:], in0=rs_sb[:],
                                         scalar1=rstat[:, 0:1], scalar2=None,
                                         op0=ALU.is_equal)
                        vv.tensor_tensor(out=w_sb[:], in0=rs_sb[:], in1=rt_sb[:],
                                         op=ALU.mult)
                        vv.tensor_tensor(out=w_sb[:], in0=rs_sb[:], in1=w_sb[:],
                                         op=ALU.subtract)
                        vv.reduce_max(out=rstat[:, 1:2], in_=w_sb[:], axis=AX.X)
                        vv.tensor_scalar(out=w_sb[:], in0=w_sb[:],
                                         scalar1=rstat[:, 1:2], scalar2=None,
                                         op0=ALU.is_equal)
                        vv.tensor_tensor(out=rt_sb[:], in0=rt_sb[:], in1=w_sb[:],
                                         op=ALU.add)
                        v_w = vv.tensor_tensor(out=w_sb[:], in0=rs_sb[:],
                                               in1=rt_sb[:], op=ALU.mult)
                        W(tn, vS, v_w)
                        ins = tn.transpose(pWT[:], w_sb[:], identf[:])
                        t_wt = tS.inc(ins)
                        W(ve, tS, t_wt)
                        vv.tensor_copy(out=W_aug[0:8, 128 * stt:128 * (stt + 1)],
                                       in_=pWT[:])
                v_W_done = vS.n
            if dbg:
                W(sy, vS, v_W_done)
                W(sy, dDBG, dDBG.n)
                dma(sy, dDBG, taps["d_W"][:], W_aug[:])
            if phase_limit < 2:
                wait_all_dma(sy)
                wait_all_dma(gp)
                return nc

            with ExitStack() as stMM:
                eMM = stMM.enter_context
                QT = [eMM(nc.sbuf_tensor(f"QT{h}", [64, T], BF16)) for h in range(2)]
                KT = [eMM(nc.sbuf_tensor(f"KT{h}", [64, T], BF16)) for h in range(2)]
                VT = eMM(nc.sbuf_tensor("VT", [128, T], BF16))
                Vaug = eMM(nc.sbuf_tensor("Vaug", [128, TT, 66], BF16))
                P2 = eMM(nc.sbuf_tensor("P2", [128, 2, 1024], BF16))
                wbuf = eMM(nc.sbuf_tensor("wbuf", [128, 2, CT, 128], BF16))
                recro = eMM(nc.sbuf_tensor("recro", [128, 512], F32))
                stA_ = eMM(nc.sbuf_tensor("stA", [128, 16], F32))
                csml = eMM(nc.sbuf_tensor("csml", [64, 2, 512], BF16))
                cunb = eMM(nc.sbuf_tensor("cunb", [65, 2, 512], F32))
                acc = eMM(nc.sbuf_tensor("acc", [128, DT, FW], F32))
                H = eMM(nc.sbuf_tensor("H", [128, 2, FT, FW], BF16))
                rw1r = eMM(nc.sbuf_tensor("rw1r", [128, 2, 2, CT, 128], BF16))
                rw2r = eMM(nc.sbuf_tensor("rw2r", [128, 2, FT, 128], BF16))
                bcw = eMM(nc.sbuf_tensor("bcw", [128, 2, FW], BF16))
                pS = [eMM(nc.psum_tensor(f"pS{i}", [128, 1024], F32)) for i in range(2)]
                pC = eMM(nc.psum_tensor("pC", [65, 512], F32))
                pH = [eMM(nc.psum_tensor("pH0", [128, 512], F32))]
                pSM = eMM(nc.psum_tensor("pSM", [128, 512], F32))
                pVb = eMM(nc.psum_tensor("pVb", [128, 512], BF16))

                W(gp, gS, g_pre)
                W(gp, vS, vS.n)
                W(gp, sS, sS.n)
                W(gp, tS, tS.n)
                wait_all_dma(gp)
                g_rows = gg.memset(Vaug[:, :, 64:65], 1.0)

                precast_attn = [(wq_d[i], wqb_d[i]) for i in range(1, NP)]
                precast_attn += [(wk_d[i], wkb_d[i]) for i in range(1, NP)]
                precast_attn += [(wv_d[i], wvb_d[i]) for i in range(1, NP)]
                precast = [w for i in range(E9)
                           for w in ((rw1_d[i], rw1b_d[i]), (rw2_d[i], rw2b_d[i]))]
                d_wcast = [0]
                d_wcast_attn = [0]

                fs = {
                    "v_pH": [0], "t_u_used": {}, "d_u": {},
                    "t_r2_used": {}, "d_r2": {}, "v_hc": {},
                    "s_gel": {}, "v_wm": {}, "t_sm": {},
                    "v_accdone": {}, "d_spill": {0: 0},
                    "v_evSM": 0, "s_evSM": 0, "s_bcw": {},
                    "nui": 2, "nsmi": 2,
                }
                NU = FT // 2

                def issue_u(ui):
                    g, r = divmod(ui, NU)
                    c, ei = divmod(g, E9)
                    if c >= FCH:
                        return
                    if ui - 2 in fs["t_u_used"]:
                        W(gp, tS, fs["t_u_used"][ui - 2])
                    W(gp, dPC, d_wcast[0])
                    fs["d_u"][ui] = dma(
                        gp, dW1[ui % 2], rw1r[:, ui % 2, :, :, :],
                        rw1b_d[ei][:, 2 * r:2 * r + 2, :, :])

                def issue_r2(smi):
                    g, d = divmod(smi, DT)
                    c, ei = divmod(g, E9)
                    if c >= FCH:
                        return
                    if smi - 2 in fs["t_r2_used"]:
                        W(gp, tS, fs["t_r2_used"][smi - 2])
                    W(gp, dPC, d_wcast[0])
                    fs["d_r2"][smi] = dma(
                        gp, dR2[smi % 2], rw2r[:, smi % 2, :, :],
                        rw2b_d[ei][:, d, :, :])

                def blk_bcw(c, ei):
                    g = c * E9 + ei
                    W(tn, vS, fs["v_wm"].get(g - 2, 0))
                    W(tn, vS, fs["v_evSM"])
                    W(tn, sS, fs["s_evSM"])
                    ins = tn.matmul(pSM[:], sel_sb[:, 128 * ei:128 * (ei + 1)],
                                    W_aug[:, FW * c:FW * (c + 1)],
                                    start=True, stop=True)
                    t_ = tS.inc(ins)
                    W(sl, tS, t_)
                    fs["s_bcw"][g] = ss.activation(
                        out=bcw[:, g % 2, :], in_=pSM[:], func=AF.Identity,
                        scale=1.0)
                    fs["s_evSM"] = fs["s_bcw"][g]

                def blk_fm(c, ei, fj):
                    g = c * E9 + ei
                    ui = g * NU + fj // 2
                    W(tn, dW1[ui % 2], fs["d_u"][ui])
                    W(tn, vS, fs["v_pH"][0])
                    for ci in range(CT):
                        ins = tn.matmul(pH[0][:],
                                        rw1r[:, ui % 2, fj % 2, ci, :],
                                        xnT[:, ci, FW * c:FW * (c + 1)],
                                        start=(ci == 0), stop=(ci == CT - 1))
                    t_fm = tS.inc(ins)
                    fs["t_u_used"][ui] = t_fm
                    if fj % 2 == 1 and fs["nui"] == ui + 2:
                        issue_u(fs["nui"])
                        fs["nui"] += 1
                    W(ve, tS, t_fm)
                    if g - 2 in fs["t_sm"]:
                        W(ve, tS, fs["t_sm"][g - 2])
                    fs["v_hc"][(g, fj)] = vv.tensor_copy(
                        out=H[:, g % 2, fj, :], in_=pH[0][:])
                    fs["v_pH"][0] = fs["v_hc"][(g, fj)]

                def flush_gelus(pend):
                    for g in pend:
                        fs.setdefault("flushed_at", {})[g] = cno[0]
                        c, ei = divmod(g, E9)
                        last_v = 0
                        for fj in range(FT):
                            W(sl, vS, fs["v_hc"][(g, fj)])
                            last_s = ss.activation(out=H[:, g % 2, fj, :],
                                                   in_=H[:, g % 2, fj, :],
                                                   func=AF.Gelu,
                                                   bias=rb1_sb[:, ei, fj:fj + 1],
                                                   scale=1.0)
                            if ei < 8:
                                W(ve, sS, last_s)
                                last_v = vv.tensor_tensor(
                                    out=H[:, g % 2, fj, :],
                                    in0=H[:, g % 2, fj, :],
                                    in1=bcw[:, g % 2, :], op=ALU.mult)
                        fs["s_gel"][g] = last_s
                        fs["v_wm"][g] = last_v

                def blk_sm(c, ei, d):
                    g = c * E9 + ei
                    smi = g * DT + d
                    W(tn, dR2[smi % 2], fs["d_r2"][smi])
                    W(tn, sS, fs["s_gel"][g])
                    W(tn, vS, fs["v_wm"][g])
                    W(tn, vS, fs["v_evSM"])
                    W(tn, sS, fs["s_evSM"])
                    for fj in range(FT):
                        ins = tn.matmul(pSM[:], rw2r[:, smi % 2, fj, :],
                                        H[:, g % 2, fj, :],
                                        start=(fj == 0), stop=(fj == FT - 1))
                    t_sm = tS.inc(ins)
                    fs["t_r2_used"][smi] = t_sm
                    if d == DT - 1:
                        fs["t_sm"][g] = t_sm
                    if fs["nsmi"] == smi + 2:
                        issue_r2(fs["nsmi"])
                        fs["nsmi"] += 1
                    W(ve, tS, t_sm)
                    if ei == 0:
                        W(ve, dSP, fs["d_spill"].get(c, 0))
                        fs["v_evSM"] = vv.tensor_copy(out=acc[:, d, :], in_=pSM[:])
                    else:
                        fs["v_evSM"] = vv.tensor_tensor(
                            out=acc[:, d, :], in0=acc[:, d, :], in1=pSM[:],
                            op=ALU.add)

                def blk_rb2s(c, d):
                    W(tn, vS, fs["v_evSM"])
                    W(tn, sS, fs["s_evSM"])
                    ins = tn.matmul(pSM[:], rb2_sb[:, 128 * d:128 * (d + 1)],
                                    W_aug[:, FW * c:FW * (c + 1)],
                                    start=True, stop=True)
                    t_ = tS.inc(ins)
                    W(ve, tS, t_)
                    fs["v_evSM"] = vv.tensor_tensor(
                        out=acc[:, d, :], in0=acc[:, d, :], in1=pSM[:], op=ALU.add)
                    if d == DT - 1:
                        fs["v_accdone"][c] = fs["v_evSM"]

                def blk_spill(c):
                    W(sy, vS, fs["v_accdone"][c])
                    for d in range(DT):
                        dma(sy, dSP, ffn_d[d][:, FW * c:FW * (c + 1)], acc[:, d, :])
                    fs["d_spill"][c + 1] = dSP.n

                fm_lane = []
                sm_lane = []
                for c in range(FCH):
                    for ei in range(E9):
                        if ei < 8:
                            fm_lane.append(("bcw", c, ei))
                        for fj in range(FT):
                            fm_lane.append(("fm", c, ei, fj))
                        for d in range(DT):
                            sm_lane.append(("sm", c, ei, d))
                    for d in range(DT):
                        sm_lane.append(("rb2s", c, d))
                    sm_lane.append(("spill", c))
                total_blocks = len(fm_lane) + len(sm_lane)
                cur = {"fm": 0, "sm": 0, "emitted": 0}
                pending_gelu = []

                def sm_ready(b):
                    fa = fs.setdefault("flushed_at", {})

                    def ready_g(g):
                        return g in fs["s_gel"] and cno[0] > fa.get(g, 1 << 30)
                    if b[0] == "sm":
                        return ready_g(b[1] * E9 + b[2])
                    if b[0] == "rb2s":
                        return ready_g(b[1] * E9 + E9 - 1)
                    if b[0] == "spill":
                        return b[1] in fs["v_accdone"]
                    return True

                def fm_ready(b):
                    g = b[1] * E9 + b[2]
                    if b[0] == "fm" and b[3] == 0:
                        if g >= 2 and (g - 2) not in fs["t_sm"]:
                            return False
                    if b[0] == "bcw":
                        if g >= 2 and (g - 2) not in fs["v_wm"]:
                            return False
                    return True

                def run_block(b):
                    kind = b[0]
                    if kind == "bcw":
                        blk_bcw(b[1], b[2])
                    elif kind == "fm":
                        blk_fm(b[1], b[2], b[3])
                        if b[3] == FT - 1:
                            pending_gelu.append(b[1] * E9 + b[2])
                    elif kind == "sm":
                        blk_sm(b[1], b[2], b[3])
                    elif kind == "rb2s":
                        blk_rb2s(b[1], b[2])
                    elif kind == "spill":
                        blk_spill(b[1])
                    cur["emitted"] += 1

                def pump(target):
                    progress = True
                    while cur["emitted"] < target and progress:
                        progress = False
                        while cur["sm"] < len(sm_lane) and cur["emitted"] < target:
                            b = sm_lane[cur["sm"]]
                            if not sm_ready(b):
                                break
                            run_block(b)
                            cur["sm"] += 1
                            progress = True
                        while cur["fm"] < len(fm_lane) and cur["emitted"] < target:
                            b = fm_lane[cur["fm"]]
                            if not fm_ready(b):
                                break
                            run_block(b)
                            cur["fm"] += 1
                            progress = True

                t_projdone = [0] * NP
                prev_exp = None
                prev_tPV = None
                v_ctx_prev = 0
                d_cs = [0, 0]
                pending_norm = []

                def do_norm(item):
                    h_idx, q_, v_rec_ = item
                    sl_ = q_ % 2
                    W(tn, vS, v_rec_)
                    W(tn, vS, fs["v_evSM"])
                    W(tn, sS, fs["s_evSM"])
                    ins_ = tn.matmul(pSM[0:64, :],
                                     ones_all[64 * sl_:64 * sl_ + 1, 0:64],
                                     recro[64 * sl_:64 * sl_ + 1, :],
                                     start=True, stop=True)
                    t_bc = tS.inc(ins_)
                    W(ve, tS, t_bc)
                    vv.tensor_copy(out=recro[0:64, :], in_=pSM[0:64, :])
                    fs["v_evSM"] = vS.n
                    W(ve, dR[sl_], d_cs[sl_])
                    v_cm = vv.tensor_tensor(
                        out=csml[:, sl_, :],
                        in0=cunb[0:64, sl_, :], in1=recro[0:64, :],
                        op=ALU.mult)
                    W(sy, vS, v_cm)
                    d_cs[sl_] = dma(sy, dR[sl_],
                                    ctx_d[h_idx][:, 512 * q_:512 * (q_ + 1)],
                                    csml[:, sl_, :])
                n_attn_chunks = NP * 2 * NCH
                chunk_no = 0
                cno = [0]
                total_kts = n_attn_chunks * 32
                kt_no = [0]
                kt_start = 16 * 32

                def _tgt():
                    k = kt_no[0]
                    if not d_wcast[0] or k < kt_start:
                        return 0
                    return total_blocks * (k - kt_start) // (total_kts - kt_start)

                def micro_pump():
                    kt_no[0] += 1
                    if kt_no[0] % 2 == 0:
                        pump(min(_tgt(), cur["emitted"] + 2))

                def fill(n=2):
                    pump(min(_tgt(), cur["emitted"] + n))

                for p in range(NP):
                    h0, h1 = 2 * p, 2 * p + 1
                    if p == 0:
                        dma(gp, dG, wbuf[:, 0, :, :], wq_d[p])
                        dma(gp, dG, wbuf[:, 1, :, :], wk_d[p])
                        wqk_sem, d_wqk = dG, dG.n
                    else:
                        W(sy, tS, t_projdone[p - 1])
                        W(sy, dPC, d_wcast_attn[0])
                        dma(sy, dWA, wbuf[:, 0, :, :], wqb_d[p])
                        dma(sy, dWA, wbuf[:, 1, :, :], wkb_d[p])
                        wqk_sem, d_wqk = dWA, dWA.n
                    W(ve, wqk_sem, d_wqk)
                    W(ve, vS, v_pre)
                    vw0 = vv.tensor_scalar(out=wbuf[:, 0, :, :], in0=wbuf[:, 0, :, :],
                                           scalar1=scol[:, 0:1], scalar2=None,
                                           op0=ALU.mult)
                    vw1 = vv.tensor_scalar(out=wbuf[:, 1, :, :], in0=wbuf[:, 1, :, :],
                                           scalar1=scol[:, 1:2], scalar2=None,
                                           op0=ALU.mult)

                    slots = [(pS[0], 0), (pS[0], 512), (pS[1], 0), (pS[1], 512)]
                    slot_tick = [sS.n] * 4
                    kch = 0

                    def proj(mat, lo_dst, hi_dst, b_sb, v_w, hi_wait=0):
                        nonlocal kch
                        first = tS.n
                        W(tn, vS, v_w)
                        W(tn, vS, v_ln_done)
                        W(tn, gS, g_rows)
                        for m in range(NCH):
                            ps, off = slots[kch % 4]
                            W(tn, sS, slot_tick[kch % 4])
                            for ci in range(CT):
                                ins = tn.matmul(
                                    ps[:, off:off + 512], wbuf[:, mat, ci, :],
                                    xnT[:, ci, 512 * m:512 * (m + 1)],
                                    start=(ci == 0), stop=(ci == CT - 1))
                            t_mm = tS.inc(ins)
                            if m == 0:
                                first = t_mm
                            W(sl, tS, t_mm)
                            if hi_wait:
                                W(sl, dRB, hi_wait)
                            ss.activation(out=lo_dst[0:64, 512 * m:512 * (m + 1)],
                                          in_=ps[0:64, off:off + 512],
                                          func=AF.Identity,
                                          bias=b_sb[0:64, h0:h0 + 1], scale=1.0)
                            slot_tick[kch % 4] = ss.activation(
                                out=hi_dst[64:128, 512 * m:512 * (m + 1)],
                                in_=ps[64:128, off:off + 512], func=AF.Identity,
                                bias=b_sb[64:128, h1:h1 + 1], scale=1.0)
                            kch += 1
                        return slot_tick[(kch - 1) % 4], first

                    s_q, t_q0 = proj(0, QT[0], VT, bq_sb, vw0)
                    W(gp, sS, s_q)
                    d_q1 = dma(gp, dRB, QT[1][:], VT[64:128, :])
                    if p == 0:
                        W(gp, tS, tS.n)
                        dma(gp, dG, wbuf[:, 0, :, :], wv_d[p])
                        wv_sem, d_wv = dG, dG.n
                    else:
                        W(sy, tS, tS.n)
                        dma(sy, dWA, wbuf[:, 0, :, :], wvb_d[p])
                        wv_sem, d_wv = dWA, dWA.n
                    W(ve, wv_sem, d_wv)
                    vwv = vv.tensor_scalar(out=wbuf[:, 0, :, :], in0=wbuf[:, 0, :, :],
                                           scalar1=scol[:, 2:3], scalar2=None,
                                           op0=ALU.mult)
                    s_k, _ = proj(1, KT[0], VT, bk_sb, vw1, hi_wait=d_q1)
                    W(gp, sS, s_k)
                    d_k1 = dma(gp, dRB, KT[1][:], VT[64:128, :])
                    s_v, _ = proj(0, VT, VT, bv_sb, vwv, hi_wait=d_k1)
                    t_projdone[p] = tS.n
                    if p == 0:
                        for src, dst in precast_attn:
                            dma(gp, dPC, dst, src)
                        precast_attn = []
                        d_wcast_attn[0] = dPC.n
                        W(gp, dPC, d_wcast_attn[0])
                        for src, dst in precast:
                            dma(gp, dPC, dst, src)
                        precast = []
                        d_wcast[0] = dPC.n
                        issue_u(0)
                        issue_u(1)
                        issue_r2(0)
                        issue_r2(1)

                    for hh in range(2):
                        qt, kt = QT[hh], KT[hh]
                        idsl = ident[0:64, 0:64] if hh == 0 else \
                            ident[64:128, 64:128]
                        v_vcp = vS.n
                        W(tn, sS, s_v)
                        for gvt in range(4):
                            fill()
                            W(tn, vS, v_vcp)
                            for j in range(8):
                                kti = 8 * gvt + j
                                ins = tn.transpose(
                                    pVb[:, 64 * j:64 * (j + 1)],
                                    VT[64 * hh:64 * (hh + 1),
                                       128 * kti:128 * (kti + 1)],
                                    idsl)
                            t_v = tS.inc(ins)
                            W(ve, tS, t_v)
                            v_vcp = vv.tensor_copy(
                                out=Vaug[:, 8 * gvt:8 * (gvt + 1), 0:64],
                                in_=pVb[:].rearrange("p (a b) -> p a b", a=8))
                        v_vaug = v_vcp

                        if hh == 1:
                            W(ve, dRB, d_k1)
                        for which, tt in ((0, qt), (1, kt)):
                            W(ve, sS, s_q if which == 0 else s_k)
                            v_am = vv.reduce_max(out=stA_[0:64, 0:1], in_=tt[:],
                                                 axis=AX.X,
                                                 apply_absolute_value=True)
                            fill()
                            W(tn, vS, v_am)
                            W(tn, vS, v_ctx_prev)
                            ins = tn.transpose(pC[0:1, 0:64], stA_[0:64, 0:1],
                                               identf[0:64, 0:64])
                            t_am = tS.inc(ins)
                            W(ve, tS, t_am)
                            vv.reduce_max(out=stA_[0:1, 8 + which:9 + which],
                                          in_=pC[0:1, 0:64], axis=AX.X)
                            fill()
                        vv.tensor_tensor(out=stA_[0:1, 10:11], in0=stA_[0:1, 8:9],
                                         in1=stA_[0:1, 9:10], op=ALU.mult)
                        v_negc = vv.tensor_scalar(out=stA_[0:1, 13:14],
                                                  in0=stA_[0:1, 10:11],
                                                  scalar1=-64.0, scalar2=None,
                                                  op0=ALU.mult)
                        W(tn, vS, v_negc)
                        W(tn, sS, sS.n)
                        ins = tn.matmul(pS[0][:, 0:1], ones_r128[:],
                                        stA_[0:1, 13:14], start=True, stop=True)
                        t_nc = tS.inc(ins)
                        W(ve, tS, t_nc)
                        W(ve, sS, sS.n)
                        v_ncol = vv.tensor_copy(out=stA_[:, 15:16],
                                                in_=pS[0][:, 0:1])

                        W(tn, vS, v_ncol)
                        W(tn, vS, v_vaug)
                        for q in range(NCH):
                            s_exp = [0] * 16
                            t_PV = [0] * 16
                            for j in range(16):
                                if j >= 2:
                                    W(tn, sS, s_exp[j - 2])
                                elif prev_exp is not None:
                                    W(tn, sS, prev_exp[14 + j])
                                for half in range(2):
                                    kt_i = 2 * j + half
                                    ins = tn.matmul(
                                        pS[j % 2][:, 512 * half:512 * (half + 1)],
                                        kt[:, 128 * kt_i:128 * (kt_i + 1)],
                                        qt[:, 512 * q:512 * (q + 1)],
                                        start=True, stop=True)
                                t_S = tS.inc(ins)
                                if j >= 1:
                                    jj = j - 1
                                    W(tn, sS, s_exp[jj])
                                    if jj == 0:
                                        W(tn, vS, v_ctx_prev)
                                    for half in range(2):
                                        kt_i = 2 * jj + half
                                        ins = tn.matmul(
                                            pC[:], Vaug[:, kt_i, 0:65],
                                            P2[:, jj % 2,
                                               512 * half:512 * (half + 1)],
                                            start=(kt_i == 0), stop=False)
                                    t_PV[jj] = tS.inc(ins)
                                W(sl, tS, t_S)
                                if j >= 2:
                                    W(sl, tS, t_PV[j - 2])
                                elif prev_tPV is not None:
                                    W(sl, tS, prev_tPV[14 + j])
                                s_exp[j] = ss.activation(
                                    out=P2[:, j % 2, :], in_=pS[j % 2][:],
                                    func=AF.Exp, bias=stA_[:, 15:16], scale=1.0)
                                if j == 4 and pending_norm:
                                    do_norm(pending_norm.pop(0))
                                micro_pump()
                                micro_pump()
                            W(tn, sS, s_exp[15])
                            for half in range(2):
                                ins = tn.matmul(pC[:], Vaug[:, 30 + half, 0:65],
                                                P2[:, 1,
                                                   512 * half:512 * (half + 1)],
                                                start=False, stop=(half == 1))
                            t_pvl = tS.inc(ins)
                            W(ve, tS, t_pvl)
                            v_ctx_prev = vv.tensor_copy(out=cunb[:, q % 2, :],
                                                        in_=pC[:])
                            v_rec = vv.reciprocal(out=recro[64 * (q % 2):
                                                            64 * (q % 2) + 1, :],
                                                  in_=cunb[64:65, q % 2, :])
                            pending_norm.append((2 * p + hh, q, v_rec))
                            prev_exp = s_exp
                            prev_tPV = t_PV
                            chunk_no += 1
                            cno[0] = chunk_no
                            if d_wcast[0] and pending_gelu:
                                flush_gelus(pending_gelu)
                                pending_gelu.clear()
                                fill(8)

                while pending_norm:
                    do_norm(pending_norm.pop(0))
                cno[0] = 1 << 31
                guard = 0
                while cur["emitted"] < total_blocks:
                    if pending_gelu:
                        flush_gelus(pending_gelu)
                        pending_gelu.clear()
                    before = cur["emitted"]
                    pump(total_blocks)
                    if cur["emitted"] == before and not pending_gelu:
                        raise RuntimeError("ffn scheduler deadlock")
                    guard += 1
                    if guard > 100000:
                        raise RuntimeError("ffn scheduler livelock")
                d_cs_done = (dR[0].n, dR[1].n)
                d_sp_done = dSP.n
                t_merged = tS.n
                v_merged = vS.n
                s_merged = sS.n

        if phase_limit < 3:
            wait_all_dma(sy)
            wait_all_dma(gp)
            return nc

        with ExitStack() as stF:
            eF = stF.enter_context
            ctx_sb = eF(nc.sbuf_tensor("ctx_sb", [128, CT, T], BF16))
            wo_sb = eF(nc.sbuf_tensor("wo_sb", [128, CT, DT, 128], BF16))
            ffn_sb = eF(nc.sbuf_tensor("ffn_sb", [128, DT, 1024], F32))
            xt2c = eF(nc.sbuf_tensor("xt2c", [128, 2, D], F32))
            ostg = eF(nc.sbuf_tensor("ostg", [128, 2, D], F32))
            pO = [eF(nc.psum_tensor(f"pO{i}", [128, 512], F32)) for i in range(2)]
            pD = [eF(nc.psum_tensor(f"pD{i}", [128, 512], F32)) for i in range(2)]

            W(gp, tS, t_merged)
            W(gp, vS, v_merged)
            W(gp, sS, s_merged)
            wait_all_dma(gp)
            dma(gp, dG, wo_sb[:], wo_d[:])
            d_wo = dG.n
            W(ve, dG, d_wo)
            v_wo = vv.tensor_scalar(
                out=wo_sb[:].rearrange("p a b c -> p (a b c)"),
                in0=wo_sb[:].rearrange("p a b c -> p (a b c)"),
                scalar1=scol[:, 3:4], scalar2=None, op0=ALU.mult)
            W(sy, tS, t_merged)
            W(sy, vS, v_merged)
            W(sy, sS, s_merged)
            W(sy, dR[0], d_cs_done[0])
            W(sy, dR[1], d_cs_done[1])
            for h in range(16):
                dma(sy, dCX, ctx_sb[64 * (h % 2):64 * (h % 2) + 64, h // 2, :],
                    ctx_d[h])
            d_ctx_l = dCX.n

            v_pDt = [0, 0]
            v_addfull = [vS.n] * 2
            d_og = [0, 0]
            t_pD_done = 0
            for c in range(4):
                W(sy, dSP, d_sp_done)
                W(sy, tS, t_pD_done)
                for d in range(DT):
                    dma(sy, dFB, ffn_sb[:, d, :],
                        ffn_d[d][:, 1024 * c:1024 * (c + 1)])
                d_fl = dFB.n
                v_ev = [vS.n] * 2
                W(tn, vS, v_wo)
                W(tn, dCX, d_ctx_l)
                for d in range(DT):
                    for m in range(2):
                        W(tn, vS, v_ev[m])
                        for ci in range(CT):
                            ins = tn.matmul(
                                pO[m][:], wo_sb[:, ci, d, :],
                                ctx_sb[:, ci, 1024 * c + 512 * m:
                                       1024 * c + 512 * (m + 1)],
                                start=(ci == 0), stop=(ci == CT - 1))
                        t_mm = tS.inc(ins)
                        W(ve, tS, t_mm)
                        W(ve, dFB, d_fl)
                        v_ev[m] = vv.scalar_tensor_tensor(
                            out=ffn_sb[:, d, 512 * m:512 * (m + 1)],
                            in0=pO[m][:], scalar=bo_sb[:, d:d + 1],
                            in1=ffn_sb[:, d, 512 * m:512 * (m + 1)],
                            op0=ALU.add, op1=ALU.add)
                v_comb = max(v_ev)
                for stt in range(8):
                    buf = stt % 2
                    row = 128 * (8 * c + stt)
                    W(sy, vS, v_addfull[buf])
                    d_xt = dma(sy, dX[buf], xt2c[:, buf, :], x_d[row:row + 128, :])
                    for r in range(2):
                        W(tn, vS, v_pDt[r])
                        W(tn, vS, v_comb)
                        for dd in range(4):
                            d = 4 * r + dd
                            ins = tn.transpose(
                                pD[r][:, 128 * dd:128 * (dd + 1)],
                                ffn_sb[:, d, 128 * stt:128 * (stt + 1)],
                                identf[:])
                        t_tp = tS.inc(ins)
                        t_pD_done = t_tp
                        W(ve, tS, t_tp)
                        W(ve, dX[buf], d_xt)
                        W(ve, dOut[buf], d_og[buf])
                        v_pDt[r] = vv.tensor_tensor(
                            out=ostg[:, buf, 512 * r:512 * (r + 1)],
                            in0=xt2c[:, buf, 512 * r:512 * (r + 1)],
                            in1=pD[r][:], op=ALU.add)
                    v_addfull[buf] = vS.n
                    W(sy, vS, v_addfull[buf])
                    d_og[buf] = dma(sy, dOut[buf], out_d[row:row + 128, :],
                                    ostg[:, buf, :])

        wait_all_dma(sy)
        wait_all_dma(gp)
    return nc


def _host_layouts(inp):
    f = np.float32

    def a(k):
        return np.asarray(inp[k], dtype=f)

    wq = a("L0_attn_w_q"); wk = a("L0_attn_w_k")
    wv = a("L0_attn_w_v"); wo = a("L0_attn_w_o")

    def wtile(w):
        return np.ascontiguousarray(
            w.T.reshape(8, 128, 8, 128).transpose(2, 1, 0, 3))

    def btile(b):
        t = b.reshape(16, 64).T
        return np.ascontiguousarray(np.concatenate([t, t], axis=0))

    rw1 = a("L0_ffn_rw1"); sw1 = a("L0_ffn_sw1")
    rw2 = a("L0_ffn_rw2"); sw2 = a("L0_ffn_sw2")
    rb1 = a("L0_ffn_rb1"); sb1 = a("L0_ffn_sb1")
    rb2 = a("L0_ffn_rb2"); sb2 = a("L0_ffn_sb2")
    rw1s = np.concatenate([rw1, sw1[None]], axis=0)
    rw2s = np.concatenate([rw2, sw2[None]], axis=0)
    rb1s = np.concatenate([rb1, sb1[None]], axis=0)
    rb2s = np.concatenate([rb2, sb2[None]], axis=0)

    shared = {
        "wq_t": wtile(wq), "wk_t": wtile(wk), "wv_t": wtile(wv),
        "wo_t": np.ascontiguousarray(
            wo.T.reshape(8, 128, 8, 128).transpose(1, 0, 2, 3)),
        "bq_t": btile(a("L0_attn_b_q")), "bk_t": btile(a("L0_attn_b_k")),
        "bv_t": btile(a("L0_attn_b_v")),
        "bo_t": np.ascontiguousarray(a("L0_attn_b_o").reshape(8, 128).T),
        "s4": np.stack([a("L0_attn_s_q").reshape(()), a("L0_attn_s_k").reshape(()),
                        a("L0_attn_s_v").reshape(()), a("L0_attn_s_o").reshape(())]
                       ).reshape(1, 4).astype(f),
        "rw1_t": np.ascontiguousarray(
            rw1s.reshape(9, 8, 128, 16, 128).transpose(0, 2, 3, 1, 4)),
        "rw2_t": np.ascontiguousarray(
            rw2s.reshape(9, 16, 128, 8, 128).transpose(0, 2, 3, 1, 4)),
        "rb1_t": np.ascontiguousarray(rb1s.reshape(9, 16, 128).transpose(2, 0, 1)),
        "rb2s": np.ascontiguousarray(rb2s),
        "rwr_t": np.ascontiguousarray(
            a("L0_ffn_router_w").reshape(8, 128, 8).transpose(1, 0, 2)),
        "rbr_t": np.ascontiguousarray(a("L0_ffn_router_b").reshape(8, 1)),
        "sel9": np.ascontiguousarray(
            np.repeat(np.eye(9, 8, dtype=f), 128, axis=1)),
    }
    x = np.asarray(inp["L0_attn_x"], dtype=f)
    return shared, x


def kernel(**inp):
    shared, x = _host_layouts(inp)
    nc = build_program(dbg=_DEBUG_TAPS)
    in_maps = []
    for b in range(8):
        m = dict(shared)
        m["x"] = np.ascontiguousarray(x[b])
        in_maps.append(m)
    res = run_bass_kernel_spmd(nc, in_maps, core_ids=list(range(8)))
    out = np.stack([np.asarray(r["out"], dtype=np.float32) for r in res.results],
                   axis=0)
    if _DEBUG_TAPS:
        kernel._last_results = res
    return out


# version 14
# speedup vs baseline: 1.0277x; 1.0038x over previous
import numpy as np
from contextlib import ExitStack

import concourse.bass as bass
import concourse.mybir as mybir
from concourse.bass_utils import run_bass_kernel_spmd

F32 = mybir.dt.float32
BF16 = mybir.dt.bfloat16
AF = mybir.ActivationFunctionType
ALU = mybir.AluOpType
AX = mybir.AxisListType

T = 4096
D = 1024
NP = 8
CT = 8
FT = 16
DT = 8
E9 = 9
NCH = 8
TT = 32
FCH = 8
FW = 512

_DEBUG_TAPS = False


class _Sem:
    def __init__(self, st, nc, name, step=1):
        self.h = st.enter_context(nc.semaphore(name))
        self.n = 0
        self.step = step

    def inc(self, instr):
        instr.then_inc(self.h, self.step)
        self.n += self.step
        return self.n


class _Waiter:
    def __init__(self):
        self.seen = {}

    def __call__(self, eng, sem, val=None):
        v = sem.n if val is None else val
        if v <= 0:
            return
        k = (id(eng), id(sem))
        if self.seen.get(k, 0) < v:
            eng.wait_ge(sem.h, v)
            self.seen[k] = v


class _Auto:

    def __init__(self, eng, sem, waiter, skip=("wait_ge", "dma_start")):
        self._e = eng
        self._s = sem
        self._W = waiter
        self._skip = set(skip)

    def __getattr__(self, name):
        f = getattr(self._e, name)
        if name in self._skip or name.startswith("_"):
            return f

        def g(*a, **k):
            self._W(self._e, self._s, self._s.n)
            ins = f(*a, **k)
            self._s.n += 1
            ins.then_inc(self._s.h, 1)
            return self._s.n

        return g


def build_program(dbg=False, phase_limit=99):
    nc = bass.Bass("TRN2", target_bir_lowering=False, debug=False)
    tn, sl, ve, gp, sy = nc.tensor, nc.scalar, nc.vector, nc.gpsimd, nc.sync

    x_d = nc.dram_tensor("x", [T, D], F32, kind="ExternalInput")
    wq_d = nc.dram_tensor("wq_t", [NP, 128, CT, 128], F32, kind="ExternalInput")
    wk_d = nc.dram_tensor("wk_t", [NP, 128, CT, 128], F32, kind="ExternalInput")
    wv_d = nc.dram_tensor("wv_t", [NP, 128, CT, 128], F32, kind="ExternalInput")
    wo_d = nc.dram_tensor("wo_t", [128, CT, DT, 128], F32, kind="ExternalInput")
    bq_d = nc.dram_tensor("bq_t", [128, 16], F32, kind="ExternalInput")
    bk_d = nc.dram_tensor("bk_t", [128, 16], F32, kind="ExternalInput")
    bv_d = nc.dram_tensor("bv_t", [128, 16], F32, kind="ExternalInput")
    bo_d = nc.dram_tensor("bo_t", [128, 8], F32, kind="ExternalInput")
    s4_d = nc.dram_tensor("s4", [1, 4], F32, kind="ExternalInput")
    rw1_d = nc.dram_tensor("rw1_t", [E9, 128, FT, CT, 128], F32, kind="ExternalInput")
    rw2_d = nc.dram_tensor("rw2_t", [E9, 128, DT, FT, 128], F32, kind="ExternalInput")
    rb1_d = nc.dram_tensor("rb1_t", [128, E9, FT], F32, kind="ExternalInput")
    rb2_d = nc.dram_tensor("rb2s", [E9, D], F32, kind="ExternalInput")
    rwr_d = nc.dram_tensor("rwr_t", [128, CT, 8], F32, kind="ExternalInput")
    rbr_d = nc.dram_tensor("rbr_t", [8, 1], F32, kind="ExternalInput")
    sel_d = nc.dram_tensor("sel9", [E9, 8 * 128], F32, kind="ExternalInput")
    out_d = nc.dram_tensor("out", [T, D], F32, kind="ExternalOutput")
    scrk = "ExternalOutput" if dbg else "Internal"
    ffn_d = nc.dram_tensor("ffn_scr", [DT, 128, T], F32, kind=scrk)
    ctx_d = nc.dram_tensor("ctx_scr", [16, 64, T], BF16, kind=scrk)
    rw1b_d = nc.dram_tensor("rw1b", [E9, 128, FT, CT, 128], BF16, kind="Internal")
    rw2b_d = nc.dram_tensor("rw2b", [E9, 128, DT, FT, 128], BF16, kind="Internal")
    wqb_d = nc.dram_tensor("wqb", [NP, 128, CT, 128], BF16, kind="Internal")
    wkb_d = nc.dram_tensor("wkb", [NP, 128, CT, 128], BF16, kind="Internal")
    wvb_d = nc.dram_tensor("wvb", [NP, 128, CT, 128], BF16, kind="Internal")
    taps = {}
    if dbg:
        taps["d_xnT"] = nc.dram_tensor("d_xnT", [128, CT, T], BF16, kind="ExternalOutput")
        taps["d_W"] = nc.dram_tensor("d_W", [E9, T], BF16, kind="ExternalOutput")

    W = _Waiter()
    with ExitStack() as st:
        e = st.enter_context
        tS = _Sem(st, nc, "tS")
        sS = _Sem(st, nc, "sS")
        vS = _Sem(st, nc, "vS")
        gS = _Sem(st, nc, "gS")
        dX = [_Sem(st, nc, f"dX{i}", 16) for i in range(2)]
        dR = [_Sem(st, nc, f"dR{i}", 16) for i in range(2)]
        dSP = _Sem(st, nc, "dSP", 16)
        dFB = _Sem(st, nc, "dFB", 16)
        dCX = _Sem(st, nc, "dCX", 16)
        dOut = [_Sem(st, nc, f"dOut{i}", 16) for i in range(2)]
        dR2 = [_Sem(st, nc, f"dR2{i}", 16) for i in range(2)]
        dWA = _Sem(st, nc, "dWA", 16)
        dDBG = _Sem(st, nc, "dDBG", 16)
        dG = _Sem(st, nc, "dG", 16)
        dRB = _Sem(st, nc, "dRB", 16)
        dW1 = [_Sem(st, nc, f"dW1{i}", 16) for i in range(2)]
        dPC = _Sem(st, nc, "dPC", 16)
        sync_sems = dX + dR + [dSP, dFB, dCX] + dOut + [dWA, dDBG]
        gp_sems = [dG, dRB, dPC] + dW1 + dR2

        def wait_all_dma(eng):
            for s_ in sync_sems + gp_sems:
                W(eng, s_, s_.n)

        def dma(q, qsem, out, in_):
            q.dma_start(out=out, in_=in_).then_inc(qsem.h, 16)
            qsem.n += 16
            return qsem.n

        vv = _Auto(ve, vS, W)
        ss = _Auto(sl, sS, W)
        gg = _Auto(gp, gS, W)

        ident = e(nc.sbuf_tensor("ident", [128, 128], BF16))
        identf = e(nc.sbuf_tensor("identf", [128, 128], F32))
        eps = e(nc.sbuf_tensor("eps", [128, 1], F32))
        ones_r128 = e(nc.sbuf_tensor("ones_r128", [1, 128], F32))
        ones_all = e(nc.sbuf_tensor("ones_all", [128, 64], F32))
        scol = e(nc.sbuf_tensor("scol", [128, 4], F32))
        s_sb = e(nc.sbuf_tensor("s_sb", [1, 4], F32))
        bq_sb = e(nc.sbuf_tensor("bq_sb", [128, 16], F32))
        bk_sb = e(nc.sbuf_tensor("bk_sb", [128, 16], F32))
        bv_sb = e(nc.sbuf_tensor("bv_sb", [128, 16], F32))
        bo_sb = e(nc.sbuf_tensor("bo_sb", [128, 8], F32))
        rwr_sb = e(nc.sbuf_tensor("rwr_sb", [128, CT, 8], BF16))
        rbr_sb = e(nc.sbuf_tensor("rbr_sb", [8, 1], F32))

        for dst, src in [(s_sb, s4_d), (bq_sb, bq_d), (bk_sb, bk_d), (bv_sb, bv_d),
                         (bo_sb, bo_d), (rbr_sb, rbr_d), (rwr_sb, rwr_d)]:
            dma(gp, dG, dst[:], src[:])
        d_pre = dG.n
        gg.memset(ident[:], 0.0)
        gg.affine_select(out=ident[:], in_=ident[:], compare_op=ALU.not_equal,
                         fill=1.0, base=0, pattern=[[-1, 128]], channel_multiplier=1)
        gg.memset(identf[:], 0.0)
        gg.affine_select(out=identf[:], in_=identf[:], compare_op=ALU.not_equal,
                         fill=1.0, base=0, pattern=[[-1, 128]], channel_multiplier=1)
        gg.memset(eps[:], 1e-5)
        gg.memset(ones_r128[:], 1.0)
        g_pre = gg.memset(ones_all[:], 1.0)

        with ExitStack() as stM:
            eM = stM.enter_context
            xnT = eM(nc.sbuf_tensor("xnT", [128, CT, T], BF16))
            W_aug = eM(nc.sbuf_tensor("W_aug", [E9, T], BF16))
            rb1_sb = eM(nc.sbuf_tensor("rb1_sb", [128, E9, FT], F32))
            rb2_sb = eM(nc.sbuf_tensor("rb2_sb", [E9, D], BF16))
            sel_sb = eM(nc.sbuf_tensor("sel_sb", [E9, 8 * 128], BF16))
            W(gp, dG, d_pre)
            dma(gp, dG, rb1_sb[:], rb1_d[:])
            dma(gp, dG, rb2_sb[:], rb2_d[:])
            dma(gp, dG, sel_sb[:], sel_d[:])
            d_pre2 = dG.n
            W(gp, gS, g_pre)
            g_wa = gg.memset(W_aug[:], 1.0)

            with ExitStack() as st0:
                e0 = st0.enter_context
                xt2 = e0(nc.sbuf_tensor("p0_x", [128, 2, D], F32))
                xnb = e0(nc.sbuf_tensor("p0_xn", [128, 2, D], BF16))
                st0_ = e0(nc.sbuf_tensor("p0_stat", [128, 8], F32))
                bns = e0(nc.sbuf_tensor("p0_bns", [128, 12], F32))
                pbf = e0(nc.psum_tensor("p0_pbf", [128, 2, 1024], BF16))
                ppre = e0(nc.psum_tensor("p0_pre", [128, 8], F32))

                W(tn, dG, d_pre)
                W(tn, gS, g_pre)
                for i in range(4):
                    ins = tn.matmul(ppre[:, i:i + 1], ones_r128[:], s_sb[0:1, i:i + 1],
                                    start=True, stop=True)
                t_pre = tS.inc(ins)
                W(ve, tS, t_pre)
                W(ve, dG, d_pre)
                vv.tensor_copy(out=scol[:], in_=ppre[:, 0:4])
                vv.tensor_scalar(out=scol[:, 0:1], in0=scol[:, 0:1], scalar1=0.125,
                                 scalar2=None, op0=ALU.mult)
                v_pre = vv.tensor_scalar(out=bq_sb[:], in0=bq_sb[:], scalar1=0.125,
                                         scalar2=None, op0=ALU.mult)

                v_xn = [0] * TT
                v_cp = [0] * TT
                d_x = [0] * TT
                d_x[0] = dma(sy, dX[0], xt2[:, 0, :], x_d[0:128, :])
                d_x[1] = dma(sy, dX[1], xt2[:, 1, :], x_d[128:256, :])
                v_stat = [0] * TT
                t_trs = [0] * TT
                for i in range(TT):
                    xt = xt2[:, i % 2, :]
                    if i >= 2:
                        W(ve, tS, t_trs[i - 2])
                        v_cp[i - 2] = vv.tensor_copy(
                            out=xnT[:, :, 128 * (i - 2):128 * (i - 1)],
                            in_=pbf[:, (i - 2) % 2, :]
                            .rearrange("p (c t) -> p c t", c=CT))
                    W(ve, dX[i % 2], d_x[i])
                    W(ve, vS, v_pre)
                    vv.bn_stats(out=bns[:, 0:6], in_=xt[:, 0:512])
                    vv.bn_stats(out=bns[:, 6:12], in_=xt[:, 512:1024])
                    vv.bn_aggr(out=st0_[:, 0:2], in_=bns[:])
                    v_st = vv.tensor_scalar(out=st0_[:, 4:5], in0=st0_[:, 0:1],
                                            scalar1=-1.0, scalar2=None,
                                            op0=ALU.mult)
                    v_stat[i] = v_st
                    W(sl, vS, v_st)
                    s_r = 0
                    ss.activation(out=st0_[:, 5:6], in_=st0_[:, 1:2], func=AF.Ln,
                                  bias=eps[:], scale=1.0)
                    s_r = ss.activation(out=st0_[:, 6:7], in_=st0_[:, 5:6],
                                        func=AF.Exp, scale=-0.5)
                    W(ve, sS, s_r)
                    v_xn[i] = vv.tensor_scalar(out=xnb[:, i % 2, :], in0=xt,
                                               scalar1=st0_[:, 4:5],
                                               scalar2=st0_[:, 6:7],
                                               op0=ALU.add, op1=ALU.mult)
                    if i + 2 < TT:
                        W(sy, vS, v_xn[i])
                        d_x[i + 2] = dma(sy, dX[i % 2], xt2[:, i % 2, :],
                                         x_d[128 * (i + 2):128 * (i + 3), :])
                    W(tn, vS, v_xn[i])
                    if i >= 2:
                        W(tn, vS, v_cp[i - 2])
                    for j in range(CT):
                        ins = tn.transpose(pbf[:, i % 2, 128 * j:128 * (j + 1)],
                                           xnb[:, i % 2, 128 * j:128 * (j + 1)],
                                           ident[:])
                    t_trs[i] = tS.inc(ins)
                for i in (TT - 2, TT - 1):
                    W(ve, tS, t_trs[i])
                    v_cp[i] = vv.tensor_copy(
                        out=xnT[:, :, 128 * i:128 * (i + 1)],
                        in_=pbf[:, i % 2, :].rearrange("p (c t) -> p c t", c=CT))
                v_ln_done = v_cp[TT - 1]

            if dbg:
                W(sy, vS, v_ln_done)
                W(sy, dDBG, dDBG.n)
                dma(sy, dDBG, taps["d_xnT"][:], xnT[:])
            if phase_limit < 1:
                wait_all_dma(sy)
                wait_all_dma(gp)
                return nc

            with ExitStack() as stR:
                eR = stR.enter_context
                rre = eR(nc.sbuf_tensor("rre", [8, 512], F32))
                rt_sb = eR(nc.sbuf_tensor("rt_sb", [128, 8], F32))
                rs_sb = eR(nc.sbuf_tensor("rs_sb", [128, 8], F32))
                w_sb = eR(nc.sbuf_tensor("w_sb", [128, 8], F32))
                rstat = eR(nc.sbuf_tensor("rstat", [128, 4], F32))
                pR = eR(nc.psum_tensor("pR", [8, 512], F32))
                pRT = eR(nc.psum_tensor("pRT", [128, 8], F32))
                pWT = eR(nc.psum_tensor("pWT", [8, 128], F32))

                s_rr = sS.n
                t_rt_last = tS.n
                W(tn, vS, v_ln_done)
                W(tn, dG, d_pre2)
                W(tn, gS, g_wa)
                for m in range(NCH):
                    W(tn, sS, s_rr)
                    for ci in range(CT):
                        ins = tn.matmul(pR[:], rwr_sb[:, ci, :],
                                        xnT[:, ci, 512 * m:512 * (m + 1)],
                                        start=(ci == 0), stop=(ci == CT - 1))
                    t_rm = tS.inc(ins)
                    W(sl, tS, t_rm)
                    W(sl, tS, t_rt_last)
                    s_rr = ss.activation(out=rre[:], in_=pR[:], func=AF.Identity,
                                         bias=rbr_sb[:], scale=1.0)
                    for k in range(4):
                        stt = 4 * m + k
                        W(tn, sS, s_rr)
                        ins = tn.transpose(pRT[:], rre[:, 128 * k:128 * (k + 1)],
                                           identf[0:8, 0:8])
                        t_rt_last = tS.inc(ins)
                        W(ve, tS, t_rt_last)
                        vv.tensor_copy(out=rt_sb[:], in_=pRT[:])
                        vv.reduce_max(out=rstat[:, 0:1], in_=rt_sb[:], axis=AX.X)
                        v_rm = vv.tensor_scalar(out=rstat[:, 1:2], in0=rstat[:, 0:1],
                                                scalar1=-1.0, scalar2=None,
                                                op0=ALU.mult)
                        W(sl, vS, v_rm)
                        s_re = ss.activation(out=rs_sb[:], in_=rt_sb[:],
                                             func=AF.Exp, bias=rstat[:, 1:2],
                                             scale=1.0)
                        W(ve, sS, s_re)
                        vv.reduce_sum(out=rstat[:, 2:3], in_=rs_sb[:], axis=AX.X)
                        vv.reciprocal(out=rstat[:, 3:4], in_=rstat[:, 2:3])
                        vv.tensor_scalar(out=rs_sb[:], in0=rs_sb[:],
                                         scalar1=rstat[:, 3:4], scalar2=None,
                                         op0=ALU.mult)
                        vv.reduce_max(out=rstat[:, 0:1], in_=rs_sb[:], axis=AX.X)
                        vv.tensor_scalar(out=rt_sb[:], in0=rs_sb[:],
                                         scalar1=rstat[:, 0:1], scalar2=None,
                                         op0=ALU.is_equal)
                        vv.tensor_tensor(out=w_sb[:], in0=rs_sb[:], in1=rt_sb[:],
                                         op=ALU.mult)
                        vv.tensor_tensor(out=w_sb[:], in0=rs_sb[:], in1=w_sb[:],
                                         op=ALU.subtract)
                        vv.reduce_max(out=rstat[:, 1:2], in_=w_sb[:], axis=AX.X)
                        vv.tensor_scalar(out=w_sb[:], in0=w_sb[:],
                                         scalar1=rstat[:, 1:2], scalar2=None,
                                         op0=ALU.is_equal)
                        vv.tensor_tensor(out=rt_sb[:], in0=rt_sb[:], in1=w_sb[:],
                                         op=ALU.add)
                        v_w = vv.tensor_tensor(out=w_sb[:], in0=rs_sb[:],
                                               in1=rt_sb[:], op=ALU.mult)
                        W(tn, vS, v_w)
                        ins = tn.transpose(pWT[:], w_sb[:], identf[:])
                        t_wt = tS.inc(ins)
                        W(ve, tS, t_wt)
                        vv.tensor_copy(out=W_aug[0:8, 128 * stt:128 * (stt + 1)],
                                       in_=pWT[:])
                v_W_done = vS.n
            if dbg:
                W(sy, vS, v_W_done)
                W(sy, dDBG, dDBG.n)
                dma(sy, dDBG, taps["d_W"][:], W_aug[:])
            if phase_limit < 2:
                wait_all_dma(sy)
                wait_all_dma(gp)
                return nc

            with ExitStack() as stMM:
                eMM = stMM.enter_context
                QT = [eMM(nc.sbuf_tensor(f"QT{h}", [64, T], BF16)) for h in range(2)]
                KT = [eMM(nc.sbuf_tensor(f"KT{h}", [64, T], BF16)) for h in range(2)]
                VT = eMM(nc.sbuf_tensor("VT", [128, T], BF16))
                Vaug = eMM(nc.sbuf_tensor("Vaug", [128, TT, 66], BF16))
                P2 = eMM(nc.sbuf_tensor("P2", [128, 2, 1024], BF16))
                wbuf = eMM(nc.sbuf_tensor("wbuf", [128, 2, CT, 128], BF16))
                recro = eMM(nc.sbuf_tensor("recro", [128, 512], F32))
                stA_ = eMM(nc.sbuf_tensor("stA", [128, 16], F32))
                csml = eMM(nc.sbuf_tensor("csml", [64, 2, 512], BF16))
                cunb = eMM(nc.sbuf_tensor("cunb", [65, 2, 512], F32))
                acc = eMM(nc.sbuf_tensor("acc", [128, DT, FW], F32))
                H = eMM(nc.sbuf_tensor("H", [128, 2, FT, FW], BF16))
                rw1r = eMM(nc.sbuf_tensor("rw1r", [128, 2, 2, CT, 128], BF16))
                rw2r = eMM(nc.sbuf_tensor("rw2r", [128, 2, FT, 128], BF16))
                bcw = eMM(nc.sbuf_tensor("bcw", [128, 2, FW], BF16))
                pS = [eMM(nc.psum_tensor(f"pS{i}", [128, 1024], F32)) for i in range(2)]
                pC = eMM(nc.psum_tensor("pC", [65, 512], F32))
                pH = [eMM(nc.psum_tensor("pH0", [128, 512], F32))]
                pSM = eMM(nc.psum_tensor("pSM", [128, 512], F32))
                pVb = eMM(nc.psum_tensor("pVb", [128, 512], BF16))

                W(gp, gS, g_pre)
                W(gp, vS, vS.n)
                W(gp, sS, sS.n)
                W(gp, tS, tS.n)
                wait_all_dma(gp)
                g_rows = gg.memset(Vaug[:, :, 64:65], 1.0)

                precast_attn = [(wq_d[i], wqb_d[i]) for i in range(1, NP)]
                precast_attn += [(wk_d[i], wkb_d[i]) for i in range(1, NP)]
                precast_attn += [(wv_d[i], wvb_d[i]) for i in range(1, NP)]
                precast = [w for i in range(E9)
                           for w in ((rw1_d[i], rw1b_d[i]), (rw2_d[i], rw2b_d[i]))]
                d_wcast = [0]
                d_wcast_attn = [0]

                fs = {
                    "v_pH": [0], "t_u_used": {}, "d_u": {},
                    "t_r2_used": {}, "d_r2": {}, "v_hc": {},
                    "s_gel": {}, "v_wm": {}, "t_sm": {},
                    "v_accdone": {}, "d_spill": {0: 0},
                    "v_evSM": 0, "s_evSM": 0, "g_evSM": 0, "s_bcw": {},
                    "nui": 2, "nsmi": 2,
                }
                NU = FT // 2

                def issue_u(ui):
                    g, r = divmod(ui, NU)
                    c, ei = divmod(g, E9)
                    if c >= FCH:
                        return
                    if ui - 2 in fs["t_u_used"]:
                        W(gp, tS, fs["t_u_used"][ui - 2])
                    W(gp, dPC, d_wcast[0])
                    fs["d_u"][ui] = dma(
                        gp, dW1[ui % 2], rw1r[:, ui % 2, :, :, :],
                        rw1b_d[ei][:, 2 * r:2 * r + 2, :, :])

                def issue_r2(smi):
                    g, d = divmod(smi, DT)
                    c, ei = divmod(g, E9)
                    if c >= FCH:
                        return
                    if smi - 2 in fs["t_r2_used"]:
                        W(gp, tS, fs["t_r2_used"][smi - 2])
                    W(gp, dPC, d_wcast[0])
                    fs["d_r2"][smi] = dma(
                        gp, dR2[smi % 2], rw2r[:, smi % 2, :, :],
                        rw2b_d[ei][:, d, :, :])

                def blk_bcw(c, ei):
                    g = c * E9 + ei
                    W(tn, vS, fs["v_wm"].get(g - 2, 0))
                    W(tn, vS, fs["v_evSM"])
                    W(tn, sS, fs["s_evSM"])
                    W(tn, gS, fs["g_evSM"])
                    ins = tn.matmul(pSM[:], sel_sb[:, 128 * ei:128 * (ei + 1)],
                                    W_aug[:, FW * c:FW * (c + 1)],
                                    start=True, stop=True)
                    t_ = tS.inc(ins)
                    W(sl, tS, t_)
                    fs["s_bcw"][g] = ss.activation(
                        out=bcw[:, g % 2, :], in_=pSM[:], func=AF.Identity,
                        scale=1.0)
                    fs["s_evSM"] = fs["s_bcw"][g]

                def blk_fm(c, ei, fj):
                    g = c * E9 + ei
                    ui = g * NU + fj // 2
                    W(tn, dW1[ui % 2], fs["d_u"][ui])
                    W(tn, vS, fs["v_pH"][0])
                    for ci in range(CT):
                        ins = tn.matmul(pH[0][:],
                                        rw1r[:, ui % 2, fj % 2, ci, :],
                                        xnT[:, ci, FW * c:FW * (c + 1)],
                                        start=(ci == 0), stop=(ci == CT - 1))
                    t_fm = tS.inc(ins)
                    fs["t_u_used"][ui] = t_fm
                    if fj % 2 == 1 and fs["nui"] == ui + 2:
                        issue_u(fs["nui"])
                        fs["nui"] += 1
                    W(ve, tS, t_fm)
                    if g - 2 in fs["t_sm"]:
                        W(ve, tS, fs["t_sm"][g - 2])
                    fs["v_hc"][(g, fj)] = vv.tensor_copy(
                        out=H[:, g % 2, fj, :], in_=pH[0][:])
                    fs["v_pH"][0] = fs["v_hc"][(g, fj)]

                def flush_gelus(pend):
                    for g in pend:
                        fs.setdefault("flushed_at", {})[g] = cno[0]
                        c, ei = divmod(g, E9)
                        last_v = 0
                        for fj in range(FT):
                            W(sl, vS, fs["v_hc"][(g, fj)])
                            last_s = ss.activation(out=H[:, g % 2, fj, :],
                                                   in_=H[:, g % 2, fj, :],
                                                   func=AF.Gelu,
                                                   bias=rb1_sb[:, ei, fj:fj + 1],
                                                   scale=1.0)
                            if ei < 8:
                                W(ve, sS, last_s)
                                last_v = vv.tensor_tensor(
                                    out=H[:, g % 2, fj, :],
                                    in0=H[:, g % 2, fj, :],
                                    in1=bcw[:, g % 2, :], op=ALU.mult)
                        fs["s_gel"][g] = last_s
                        fs["v_wm"][g] = last_v

                def blk_sm(c, ei, d):
                    g = c * E9 + ei
                    smi = g * DT + d
                    W(tn, dR2[smi % 2], fs["d_r2"][smi])
                    W(tn, sS, fs["s_gel"][g])
                    W(tn, vS, fs["v_wm"][g])
                    W(tn, vS, fs["v_evSM"])
                    W(tn, sS, fs["s_evSM"])
                    W(tn, gS, fs["g_evSM"])
                    for fj in range(FT):
                        ins = tn.matmul(pSM[:], rw2r[:, smi % 2, fj, :],
                                        H[:, g % 2, fj, :],
                                        start=(fj == 0), stop=(fj == FT - 1))
                    t_sm = tS.inc(ins)
                    fs["t_r2_used"][smi] = t_sm
                    if d == DT - 1:
                        fs["t_sm"][g] = t_sm
                    if fs["nsmi"] == smi + 2:
                        issue_r2(fs["nsmi"])
                        fs["nsmi"] += 1
                    W(ve, tS, t_sm)
                    if ei == 0:
                        W(ve, dSP, fs["d_spill"].get(c, 0))
                        fs["v_evSM"] = vv.tensor_copy(out=acc[:, d, :], in_=pSM[:])
                    else:
                        fs["v_evSM"] = vv.tensor_tensor(
                            out=acc[:, d, :], in0=acc[:, d, :], in1=pSM[:],
                            op=ALU.add)

                def blk_rb2s(c, d):
                    W(tn, vS, fs["v_evSM"])
                    W(tn, sS, fs["s_evSM"])
                    W(tn, gS, fs["g_evSM"])
                    ins = tn.matmul(pSM[:], rb2_sb[:, 128 * d:128 * (d + 1)],
                                    W_aug[:, FW * c:FW * (c + 1)],
                                    start=True, stop=True)
                    t_ = tS.inc(ins)
                    W(ve, tS, t_)
                    fs["v_evSM"] = vv.tensor_tensor(
                        out=acc[:, d, :], in0=acc[:, d, :], in1=pSM[:], op=ALU.add)
                    if d == DT - 1:
                        fs["v_accdone"][c] = fs["v_evSM"]

                def blk_spill(c):
                    W(sy, vS, fs["v_accdone"][c])
                    for d in range(DT):
                        dma(sy, dSP, ffn_d[d][:, FW * c:FW * (c + 1)], acc[:, d, :])
                    fs["d_spill"][c + 1] = dSP.n

                fm_lane = []
                sm_lane = []
                for c in range(FCH):
                    for ei in range(E9):
                        if ei < 8:
                            fm_lane.append(("bcw", c, ei))
                        for fj in range(FT):
                            fm_lane.append(("fm", c, ei, fj))
                        for d in range(DT):
                            sm_lane.append(("sm", c, ei, d))
                    for d in range(DT):
                        sm_lane.append(("rb2s", c, d))
                    sm_lane.append(("spill", c))
                total_blocks = len(fm_lane) + len(sm_lane)
                cur = {"fm": 0, "sm": 0, "emitted": 0}
                pending_gelu = []

                def sm_ready(b):
                    fa = fs.setdefault("flushed_at", {})

                    def ready_g(g):
                        return g in fs["s_gel"] and cno[0] > fa.get(g, 1 << 30)
                    if b[0] == "sm":
                        return ready_g(b[1] * E9 + b[2])
                    if b[0] == "rb2s":
                        return ready_g(b[1] * E9 + E9 - 1)
                    if b[0] == "spill":
                        return b[1] in fs["v_accdone"]
                    return True

                def fm_ready(b):
                    g = b[1] * E9 + b[2]
                    if b[0] == "fm" and b[3] == 0:
                        if g >= 2 and (g - 2) not in fs["t_sm"]:
                            return False
                    if b[0] == "bcw":
                        if g >= 2 and (g - 2) not in fs["v_wm"]:
                            return False
                    return True

                def run_block(b):
                    kind = b[0]
                    if kind == "bcw":
                        blk_bcw(b[1], b[2])
                    elif kind == "fm":
                        blk_fm(b[1], b[2], b[3])
                        if b[3] == FT - 1:
                            pending_gelu.append(b[1] * E9 + b[2])
                    elif kind == "sm":
                        blk_sm(b[1], b[2], b[3])
                    elif kind == "rb2s":
                        blk_rb2s(b[1], b[2])
                    elif kind == "spill":
                        blk_spill(b[1])
                    cur["emitted"] += 1

                def pump(target):
                    progress = True
                    while cur["emitted"] < target and progress:
                        progress = False
                        while cur["sm"] < len(sm_lane) and cur["emitted"] < target:
                            b = sm_lane[cur["sm"]]
                            if not sm_ready(b):
                                break
                            run_block(b)
                            cur["sm"] += 1
                            progress = True
                        while cur["fm"] < len(fm_lane) and cur["emitted"] < target:
                            b = fm_lane[cur["fm"]]
                            if not fm_ready(b):
                                break
                            run_block(b)
                            cur["fm"] += 1
                            progress = True

                t_projdone = [0] * NP
                prev_exp = None
                prev_tPV = None
                v_ctx_prev = 0
                cunb_used = [0, 0]
                d_cs = [0, 0]
                pending_norm = []

                def do_norm(item):
                    h_idx, q_, v_rec_ = item
                    sl_ = q_ % 2
                    W(tn, vS, v_rec_)
                    W(tn, vS, fs["v_evSM"])
                    W(tn, sS, fs["s_evSM"])
                    W(tn, gS, fs["g_evSM"])
                    ins_ = tn.matmul(pSM[0:64, :],
                                     ones_all[64 * sl_:64 * sl_ + 1, 0:64],
                                     recro[64 * sl_:64 * sl_ + 1, :],
                                     start=True, stop=True)
                    t_bc = tS.inc(ins_)
                    W(ve, tS, t_bc)
                    vv.tensor_copy(out=recro[0:64, :], in_=pSM[0:64, :])
                    fs["v_evSM"] = vS.n
                    W(ve, dR[sl_], d_cs[sl_])
                    v_cm = vv.tensor_tensor(
                        out=csml[:, sl_, :],
                        in0=cunb[0:64, sl_, :], in1=recro[0:64, :],
                        op=ALU.mult)
                    cunb_used[sl_] = v_cm
                    W(sy, vS, v_cm)
                    d_cs[sl_] = dma(sy, dR[sl_],
                                    ctx_d[h_idx][:, 512 * q_:512 * (q_ + 1)],
                                    csml[:, sl_, :])
                n_attn_chunks = NP * 2 * NCH
                chunk_no = 0
                cno = [0]
                total_kts = n_attn_chunks * 32
                kt_no = [0]
                kt_start = 16 * 32

                def _tgt():
                    k = kt_no[0]
                    if not d_wcast[0] or k < kt_start:
                        return 0
                    return total_blocks * (k - kt_start) // (total_kts - kt_start)

                def micro_pump():
                    kt_no[0] += 1
                    if kt_no[0] % 2 == 0:
                        pump(min(_tgt(), cur["emitted"] + 2))

                def fill(n=2):
                    pump(min(_tgt(), cur["emitted"] + n))

                for p in range(NP):
                    h0, h1 = 2 * p, 2 * p + 1
                    if p == 0:
                        dma(gp, dG, wbuf[:, 0, :, :], wq_d[p])
                        dma(gp, dG, wbuf[:, 1, :, :], wk_d[p])
                        wqk_sem, d_wqk = dG, dG.n
                    else:
                        W(sy, tS, t_projdone[p - 1])
                        W(sy, dPC, d_wcast_attn[0])
                        dma(sy, dWA, wbuf[:, 0, :, :], wqb_d[p])
                        dma(sy, dWA, wbuf[:, 1, :, :], wkb_d[p])
                        wqk_sem, d_wqk = dWA, dWA.n
                    W(ve, wqk_sem, d_wqk)
                    W(ve, vS, v_pre)
                    vw0 = vv.tensor_scalar(out=wbuf[:, 0, :, :], in0=wbuf[:, 0, :, :],
                                           scalar1=scol[:, 0:1], scalar2=None,
                                           op0=ALU.mult)
                    vw1 = vv.tensor_scalar(out=wbuf[:, 1, :, :], in0=wbuf[:, 1, :, :],
                                           scalar1=scol[:, 1:2], scalar2=None,
                                           op0=ALU.mult)

                    slots = [(pS[0], 0), (pS[0], 512), (pS[1], 0), (pS[1], 512)]
                    slot_tick = [sS.n] * 4
                    kch = 0

                    def proj(mat, lo_dst, hi_dst, b_sb, v_w, hi_wait=0):
                        nonlocal kch
                        first = tS.n
                        W(tn, vS, v_w)
                        W(tn, vS, v_ln_done)
                        W(tn, gS, g_rows)
                        for m in range(NCH):
                            ps, off = slots[kch % 4]
                            W(tn, sS, slot_tick[kch % 4])
                            for ci in range(CT):
                                ins = tn.matmul(
                                    ps[:, off:off + 512], wbuf[:, mat, ci, :],
                                    xnT[:, ci, 512 * m:512 * (m + 1)],
                                    start=(ci == 0), stop=(ci == CT - 1))
                            t_mm = tS.inc(ins)
                            if m == 0:
                                first = t_mm
                            W(sl, tS, t_mm)
                            if hi_wait:
                                W(sl, dRB, hi_wait)
                            ss.activation(out=lo_dst[0:64, 512 * m:512 * (m + 1)],
                                          in_=ps[0:64, off:off + 512],
                                          func=AF.Identity,
                                          bias=b_sb[0:64, h0:h0 + 1], scale=1.0)
                            slot_tick[kch % 4] = ss.activation(
                                out=hi_dst[64:128, 512 * m:512 * (m + 1)],
                                in_=ps[64:128, off:off + 512], func=AF.Identity,
                                bias=b_sb[64:128, h1:h1 + 1], scale=1.0)
                            kch += 1
                        return slot_tick[(kch - 1) % 4], first

                    s_q, t_q0 = proj(0, QT[0], VT, bq_sb, vw0)
                    W(gp, sS, s_q)
                    d_q1 = dma(gp, dRB, QT[1][:], VT[64:128, :])
                    if p == 0:
                        W(gp, tS, tS.n)
                        dma(gp, dG, wbuf[:, 0, :, :], wv_d[p])
                        wv_sem, d_wv = dG, dG.n
                    else:
                        W(sy, tS, tS.n)
                        dma(sy, dWA, wbuf[:, 0, :, :], wvb_d[p])
                        wv_sem, d_wv = dWA, dWA.n
                    W(ve, wv_sem, d_wv)
                    vwv = vv.tensor_scalar(out=wbuf[:, 0, :, :], in0=wbuf[:, 0, :, :],
                                           scalar1=scol[:, 2:3], scalar2=None,
                                           op0=ALU.mult)
                    s_k, _ = proj(1, KT[0], VT, bk_sb, vw1, hi_wait=d_q1)
                    W(gp, sS, s_k)
                    d_k1 = dma(gp, dRB, KT[1][:], VT[64:128, :])
                    s_v, _ = proj(0, VT, VT, bv_sb, vwv, hi_wait=d_k1)
                    t_projdone[p] = tS.n
                    if p == 0:
                        for src, dst in precast_attn:
                            dma(gp, dPC, dst, src)
                        precast_attn = []
                        d_wcast_attn[0] = dPC.n
                        W(gp, dPC, d_wcast_attn[0])
                        for src, dst in precast:
                            dma(gp, dPC, dst, src)
                        precast = []
                        d_wcast[0] = dPC.n
                        issue_u(0)
                        issue_u(1)
                        issue_r2(0)
                        issue_r2(1)

                    for hh in range(2):
                        qt, kt = QT[hh], KT[hh]
                        idsl = ident[0:64, 0:64] if hh == 0 else \
                            ident[64:128, 64:128]
                        v_vcp = vS.n
                        W(tn, sS, s_v)
                        for gvt in range(4):
                            fill()
                            W(tn, vS, v_vcp)
                            for j in range(8):
                                kti = 8 * gvt + j
                                ins = tn.transpose(
                                    pVb[:, 64 * j:64 * (j + 1)],
                                    VT[64 * hh:64 * (hh + 1),
                                       128 * kti:128 * (kti + 1)],
                                    idsl)
                            t_v = tS.inc(ins)
                            W(ve, tS, t_v)
                            v_vcp = vv.tensor_copy(
                                out=Vaug[:, 8 * gvt:8 * (gvt + 1), 0:64],
                                in_=pVb[:].rearrange("p (a b) -> p a b", a=8))
                        v_vaug = v_vcp

                        if hh == 1:
                            W(ve, dRB, d_k1)
                        for which, tt in ((0, qt), (1, kt)):
                            W(ve, sS, s_q if which == 0 else s_k)
                            v_am = vv.reduce_max(out=stA_[0:64, 0:1], in_=tt[:],
                                                 axis=AX.X,
                                                 apply_absolute_value=True)
                            fill()
                            W(tn, vS, v_am)
                            W(tn, sS, v_ctx_prev)
                            ins = tn.transpose(pC[0:1, 0:64], stA_[0:64, 0:1],
                                               identf[0:64, 0:64])
                            t_am = tS.inc(ins)
                            W(ve, tS, t_am)
                            vv.reduce_max(out=stA_[0:1, 8 + which:9 + which],
                                          in_=pC[0:1, 0:64], axis=AX.X)
                            fill()
                        vv.tensor_tensor(out=stA_[0:1, 10:11], in0=stA_[0:1, 8:9],
                                         in1=stA_[0:1, 9:10], op=ALU.mult)
                        v_negc = vv.tensor_scalar(out=stA_[0:1, 13:14],
                                                  in0=stA_[0:1, 10:11],
                                                  scalar1=-64.0, scalar2=None,
                                                  op0=ALU.mult)
                        W(tn, vS, v_negc)
                        W(tn, sS, sS.n)
                        ins = tn.matmul(pS[0][:, 0:1], ones_r128[:],
                                        stA_[0:1, 13:14], start=True, stop=True)
                        t_nc = tS.inc(ins)
                        W(ve, tS, t_nc)
                        W(ve, sS, sS.n)
                        v_ncol = vv.tensor_copy(out=stA_[:, 15:16],
                                                in_=pS[0][:, 0:1])

                        W(tn, vS, v_ncol)
                        W(tn, vS, v_vaug)
                        for q in range(NCH):
                            s_exp = [0] * 16
                            t_PV = [0] * 16
                            for j in range(16):
                                if j >= 2:
                                    W(tn, sS, s_exp[j - 2])
                                elif prev_exp is not None:
                                    W(tn, sS, prev_exp[14 + j])
                                for half in range(2):
                                    kt_i = 2 * j + half
                                    ins = tn.matmul(
                                        pS[j % 2][:, 512 * half:512 * (half + 1)],
                                        kt[:, 128 * kt_i:128 * (kt_i + 1)],
                                        qt[:, 512 * q:512 * (q + 1)],
                                        start=True, stop=True)
                                t_S = tS.inc(ins)
                                if j >= 1:
                                    jj = j - 1
                                    W(tn, sS, s_exp[jj])
                                    if jj == 0:
                                        W(tn, sS, v_ctx_prev)
                                    for half in range(2):
                                        kt_i = 2 * jj + half
                                        ins = tn.matmul(
                                            pC[:], Vaug[:, kt_i, 0:65],
                                            P2[:, jj % 2,
                                               512 * half:512 * (half + 1)],
                                            start=(kt_i == 0), stop=False)
                                    t_PV[jj] = tS.inc(ins)
                                W(sl, tS, t_S)
                                if j >= 2:
                                    W(sl, tS, t_PV[j - 2])
                                elif prev_tPV is not None:
                                    W(sl, tS, prev_tPV[14 + j])
                                s_exp[j] = ss.activation(
                                    out=P2[:, j % 2, :], in_=pS[j % 2][:],
                                    func=AF.Exp, bias=stA_[:, 15:16], scale=1.0)
                                if j == 4 and pending_norm:
                                    do_norm(pending_norm.pop(0))
                                micro_pump()
                                micro_pump()
                            W(tn, sS, s_exp[15])
                            for half in range(2):
                                ins = tn.matmul(pC[:], Vaug[:, 30 + half, 0:65],
                                                P2[:, 1,
                                                   512 * half:512 * (half + 1)],
                                                start=False, stop=(half == 1))
                            t_pvl = tS.inc(ins)
                            W(sl, tS, t_pvl)
                            W(sl, vS, cunb_used[q % 2])
                            v_ctx_prev = ss.activation(out=cunb[:, q % 2, :],
                                                       in_=pC[:],
                                                       func=AF.Identity,
                                                       scale=1.0)
                            W(ve, sS, v_ctx_prev)
                            v_rec = vv.reciprocal(out=recro[64 * (q % 2):
                                                            64 * (q % 2) + 1, :],
                                                  in_=cunb[64:65, q % 2, :])
                            pending_norm.append((2 * p + hh, q, v_rec))
                            prev_exp = s_exp
                            prev_tPV = t_PV
                            chunk_no += 1
                            cno[0] = chunk_no
                            if d_wcast[0] and pending_gelu:
                                flush_gelus(pending_gelu)
                                pending_gelu.clear()
                                fill(8)

                while pending_norm:
                    do_norm(pending_norm.pop(0))
                cno[0] = 1 << 31
                guard = 0
                while cur["emitted"] < total_blocks:
                    if pending_gelu:
                        flush_gelus(pending_gelu)
                        pending_gelu.clear()
                    before = cur["emitted"]
                    pump(total_blocks)
                    if cur["emitted"] == before and not pending_gelu:
                        raise RuntimeError("ffn scheduler deadlock")
                    guard += 1
                    if guard > 100000:
                        raise RuntimeError("ffn scheduler livelock")
                d_cs_done = (dR[0].n, dR[1].n)
                d_sp_done = dSP.n
                t_merged = tS.n
                v_merged = vS.n
                s_merged = sS.n

        if phase_limit < 3:
            wait_all_dma(sy)
            wait_all_dma(gp)
            return nc

        with ExitStack() as stF:
            eF = stF.enter_context
            ctx_sb = eF(nc.sbuf_tensor("ctx_sb", [128, CT, T], BF16))
            wo_sb = eF(nc.sbuf_tensor("wo_sb", [128, CT, DT, 128], BF16))
            ffn_sb = eF(nc.sbuf_tensor("ffn_sb", [128, DT, 1024], F32))
            xt2c = eF(nc.sbuf_tensor("xt2c", [128, 2, D], F32))
            ostg = eF(nc.sbuf_tensor("ostg", [128, 2, D], F32))
            pO = [eF(nc.psum_tensor(f"pO{i}", [128, 512], F32)) for i in range(2)]
            pD = [eF(nc.psum_tensor(f"pD{i}", [128, 512], F32)) for i in range(2)]

            W(gp, tS, t_merged)
            W(gp, vS, v_merged)
            W(gp, sS, s_merged)
            wait_all_dma(gp)
            dma(gp, dG, wo_sb[:], wo_d[:])
            d_wo = dG.n
            W(ve, dG, d_wo)
            v_wo = vv.tensor_scalar(
                out=wo_sb[:].rearrange("p a b c -> p (a b c)"),
                in0=wo_sb[:].rearrange("p a b c -> p (a b c)"),
                scalar1=scol[:, 3:4], scalar2=None, op0=ALU.mult)
            W(sy, tS, t_merged)
            W(sy, vS, v_merged)
            W(sy, sS, s_merged)
            W(sy, dR[0], d_cs_done[0])
            W(sy, dR[1], d_cs_done[1])
            for h in range(16):
                dma(sy, dCX, ctx_sb[64 * (h % 2):64 * (h % 2) + 64, h // 2, :],
                    ctx_d[h])
            d_ctx_l = dCX.n

            v_pDt = [0, 0]
            v_addfull = [vS.n] * 2
            d_og = [0, 0]
            t_pD_done = 0
            for c in range(4):
                W(sy, dSP, d_sp_done)
                W(sy, tS, t_pD_done)
                for d in range(DT):
                    dma(sy, dFB, ffn_sb[:, d, :],
                        ffn_d[d][:, 1024 * c:1024 * (c + 1)])
                d_fl = dFB.n
                v_ev = [vS.n] * 2
                W(tn, vS, v_wo)
                W(tn, dCX, d_ctx_l)
                for d in range(DT):
                    for m in range(2):
                        W(tn, vS, v_ev[m])
                        for ci in range(CT):
                            ins = tn.matmul(
                                pO[m][:], wo_sb[:, ci, d, :],
                                ctx_sb[:, ci, 1024 * c + 512 * m:
                                       1024 * c + 512 * (m + 1)],
                                start=(ci == 0), stop=(ci == CT - 1))
                        t_mm = tS.inc(ins)
                        W(ve, tS, t_mm)
                        W(ve, dFB, d_fl)
                        v_ev[m] = vv.scalar_tensor_tensor(
                            out=ffn_sb[:, d, 512 * m:512 * (m + 1)],
                            in0=pO[m][:], scalar=bo_sb[:, d:d + 1],
                            in1=ffn_sb[:, d, 512 * m:512 * (m + 1)],
                            op0=ALU.add, op1=ALU.add)
                v_comb = max(v_ev)
                for stt in range(8):
                    buf = stt % 2
                    row = 128 * (8 * c + stt)
                    W(sy, vS, v_addfull[buf])
                    d_xt = dma(sy, dX[buf], xt2c[:, buf, :], x_d[row:row + 128, :])
                    for r in range(2):
                        W(tn, vS, v_pDt[r])
                        W(tn, vS, v_comb)
                        for dd in range(4):
                            d = 4 * r + dd
                            ins = tn.transpose(
                                pD[r][:, 128 * dd:128 * (dd + 1)],
                                ffn_sb[:, d, 128 * stt:128 * (stt + 1)],
                                identf[:])
                        t_tp = tS.inc(ins)
                        t_pD_done = t_tp
                        W(ve, tS, t_tp)
                        W(ve, dX[buf], d_xt)
                        W(ve, dOut[buf], d_og[buf])
                        v_pDt[r] = vv.tensor_tensor(
                            out=ostg[:, buf, 512 * r:512 * (r + 1)],
                            in0=xt2c[:, buf, 512 * r:512 * (r + 1)],
                            in1=pD[r][:], op=ALU.add)
                    v_addfull[buf] = vS.n
                    W(sy, vS, v_addfull[buf])
                    d_og[buf] = dma(sy, dOut[buf], out_d[row:row + 128, :],
                                    ostg[:, buf, :])

        wait_all_dma(sy)
        wait_all_dma(gp)
    return nc


def _host_layouts(inp):
    f = np.float32

    def a(k):
        return np.asarray(inp[k], dtype=f)

    wq = a("L0_attn_w_q"); wk = a("L0_attn_w_k")
    wv = a("L0_attn_w_v"); wo = a("L0_attn_w_o")

    def wtile(w):
        return np.ascontiguousarray(
            w.T.reshape(8, 128, 8, 128).transpose(2, 1, 0, 3))

    def btile(b):
        t = b.reshape(16, 64).T
        return np.ascontiguousarray(np.concatenate([t, t], axis=0))

    rw1 = a("L0_ffn_rw1"); sw1 = a("L0_ffn_sw1")
    rw2 = a("L0_ffn_rw2"); sw2 = a("L0_ffn_sw2")
    rb1 = a("L0_ffn_rb1"); sb1 = a("L0_ffn_sb1")
    rb2 = a("L0_ffn_rb2"); sb2 = a("L0_ffn_sb2")
    rw1s = np.concatenate([rw1, sw1[None]], axis=0)
    rw2s = np.concatenate([rw2, sw2[None]], axis=0)
    rb1s = np.concatenate([rb1, sb1[None]], axis=0)
    rb2s = np.concatenate([rb2, sb2[None]], axis=0)

    shared = {
        "wq_t": wtile(wq), "wk_t": wtile(wk), "wv_t": wtile(wv),
        "wo_t": np.ascontiguousarray(
            wo.T.reshape(8, 128, 8, 128).transpose(1, 0, 2, 3)),
        "bq_t": btile(a("L0_attn_b_q")), "bk_t": btile(a("L0_attn_b_k")),
        "bv_t": btile(a("L0_attn_b_v")),
        "bo_t": np.ascontiguousarray(a("L0_attn_b_o").reshape(8, 128).T),
        "s4": np.stack([a("L0_attn_s_q").reshape(()), a("L0_attn_s_k").reshape(()),
                        a("L0_attn_s_v").reshape(()), a("L0_attn_s_o").reshape(())]
                       ).reshape(1, 4).astype(f),
        "rw1_t": np.ascontiguousarray(
            rw1s.reshape(9, 8, 128, 16, 128).transpose(0, 2, 3, 1, 4)),
        "rw2_t": np.ascontiguousarray(
            rw2s.reshape(9, 16, 128, 8, 128).transpose(0, 2, 3, 1, 4)),
        "rb1_t": np.ascontiguousarray(rb1s.reshape(9, 16, 128).transpose(2, 0, 1)),
        "rb2s": np.ascontiguousarray(rb2s),
        "rwr_t": np.ascontiguousarray(
            a("L0_ffn_router_w").reshape(8, 128, 8).transpose(1, 0, 2)),
        "rbr_t": np.ascontiguousarray(a("L0_ffn_router_b").reshape(8, 1)),
        "sel9": np.ascontiguousarray(
            np.repeat(np.eye(9, 8, dtype=f), 128, axis=1)),
    }
    x = np.asarray(inp["L0_attn_x"], dtype=f)
    return shared, x


def kernel(**inp):
    shared, x = _host_layouts(inp)
    nc = build_program(dbg=_DEBUG_TAPS)
    in_maps = []
    for b in range(8):
        m = dict(shared)
        m["x"] = np.ascontiguousarray(x[b])
        in_maps.append(m)
    res = run_bass_kernel_spmd(nc, in_maps, core_ids=list(range(8)))
    out = np.stack([np.asarray(r["out"], dtype=np.float32) for r in res.results],
                   axis=0)
    if _DEBUG_TAPS:
        kernel._last_results = res
    return out
